# version 06
# baseline (speedup 1.0000x reference)
import os
os.environ.setdefault("GAUGE_NTFF_IGNORE_DMA", "1")
import numpy as np
import ml_dtypes
import concourse.bass as bass
import concourse.mybir as mybir
from concourse.bass_utils import run_bass_kernel_spmd

F32 = mybir.dt.float32
BF16 = mybir.dt.bfloat16
AF = mybir.ActivationFunctionType
ALU = mybir.AluOpType
AX = mybir.AxisListType.X

D = 1024; NCH = 8; H = 4; DK = 256; EXP = 8; FF = 4096
TT = 256
STW = 2 * TT
CH = 128
EPS = 1e-5


def build(L, taps=()):
    NST = L // TT
    tapset = set(taps)
    nc = bass.Bass("TRN2", target_bir_lowering=False, debug=False)

    dri = lambda n, s: nc.dram_tensor(n, s, F32, kind="ExternalInput").ap()
    x_t = dri("x_t", [2, D, L])
    wq = dri("wq", [D, D]); wk = dri("wk", [D, D]); wv = dri("wv", [D, D])
    wo = dri("wo", [D, D])
    wg1 = dri("wg1", [D, D]); wg1b = dri("wg1b", [16, D])
    wog1 = dri("wog1", [D, 512]); wog1b = dri("wog1b", [8, 512])
    wb = dri("wb", [D, H]); wres = dri("wres", [D, H]); wg2 = dri("wg2", [D, 4])
    wog2 = dri("wog2", [128, 4]); gwT = dri("gwT", [D, EXP])
    cdq = dri("cdq", [NCH, 4, 128, 128]); cdk = dri("cdk", [NCH, 4, 128, 128])
    cdv = dri("cdv", [NCH, 4, 128, 128])
    cds = dri("cds", [NCH, 5, 128, 128]); cdl = dri("cdl", [NCH, 64, 128, 128])
    w1T = dri("w1T", [EXP, D, FF])
    w2r = dri("w2r", [EXP, NCH, FF, 128])
    bg1 = dri("bg1", [128, NCH]); bog1 = dri("bog1", [128, 4])
    bg2r = dri("bg2r", [128, 4]); nw = dri("nw", [128, 2])
    bres_i = dri("bres_i", [H, 1]); bog2_i = dri("bog2_i", [1, 1])
    crl_i = dri("crl_i", [H, 1]); ltemp_i = dri("ltemp_i", [1, 1])
    ident_i = dri("ident_i", [128, 128])
    masksl_i = dri("masksl_i", [128, 128]); masksu_i = dri("masksu_i", [128, 128])
    maskui_i = dri("maskui_i", [128, 128])
    sumsel_i = dri("sumsel_i", [12, 128, 68])
    rowsel4_i = dri("rowsel4_i", [4, 4, 128]); rowsel8_i = dri("rowsel8_i", [8, 8, 128])
    ones1_i = dri("ones1_i", [1, 128]); onescol_i = dri("onescol_i", [128, 1])
    statc4_i = dri("statc4_i", [4, 4 * 5 * 16])
    statc2_i = dri("statc2_i", [4, 2 * 5 * 8])
    epsc_i = dri("epsc_i", [4, 2])

    out_t = nc.dram_tensor("out_t", [2, D, L], F32, kind="ExternalOutput").ap()
    drs = lambda n, s: nc.dram_tensor(n, s, BF16, kind="Internal").ap()
    sc_attn = drs("sc_attn", [5, 128, 8192])
    sc_wog1 = drs("sc_wog1", [128, 4096])
    sc_cdqkv = drs("sc_cdqkv", [3, 128, 4096])
    sc_cdl = drs("sc_cdl", [8, 128, 8192])
    sc_cds = drs("sc_cds", [8, 128, 640])
    sc_w1 = drs("sc_w1", [EXP * 4, 128, 8192])
    sc_w2 = drs("sc_w2", [EXP * 8, 128, 4096])
    tap_drams = {}

    sb = lambda n, p, w, dt: nc.alloc_sbuf_tensor(n, [p, w], dt)
    ident = sb("ident", 128, 128, BF16)
    masksl = sb("masksl", 128, 128, BF16); masksu = sb("masksu", 128, 128, BF16)
    maskui = sb("maskui", 128, 128, BF16)
    sumsel = sb("sumsel", 128, 12 * 68, BF16)
    rowsel4 = sb("rowsel4", 4, 4 * 128, BF16); rowsel8 = sb("rowsel8", 8, 8 * 128, BF16)
    ones1f = sb("ones1f", 1, 128, F32)
    onescol = sb("onescol", 128, 1, BF16)
    statc4 = sb("statc4", 4, 4 * 5 * 16, BF16); statc2 = sb("statc2", 4, 2 * 5 * 8, BF16)
    epsc = sb("epsc", 4, 2, F32)
    wb_s = sb("wb_s", 128, NCH * 4, BF16); wres_s = sb("wres_s", 128, NCH * 4, BF16)
    wg2_s = sb("wg2_s", 128, NCH * 4, BF16); gwT_s = sb("gwT_s", 128, NCH * 8, BF16)
    wog2_s = sb("wog2_s", 128, 4, BF16)
    wg1b_s = sb("wg1b_s", 16, D, BF16); wog1b_s = sb("wog1b_s", 8, 512, BF16)
    bg1_s = sb("bg1_s", 128, NCH, F32); bog1_s = sb("bog1_s", 128, 4, F32)
    bg2r_s = sb("bg2r_s", 128, 4, F32); nw_s = sb("nw_s", 128, 2, F32)
    bres_s = sb("bres_s", H, 1, F32); bog2_s = sb("bog2_s", 1, 1, F32)
    crl_s = sb("crl_s", H, 1, F32); ltemp_s = sb("ltemp_s", 1, 1, F32)
    static_s = sb("static_s", H, 1, F32)
    tinv_s = sb("tinv_s", 1, 1, F32); tinv_bc = sb("tinv_bc", 128, 1, F32)
    tmp11 = sb("tmp11", 1, 1, F32); tmp11b = sb("tmp11b", 1, 1, F32)
    tmp11c = sb("tmp11c", 1, 1, F32); tmp11d = sb("tmp11d", 1, 1, F32)
    S_f = sb("S_f", 128, 8 * 512, F32)
    S_b = sb("S_b", 128, 8 * 512, BF16)
    rawhalo = sb("rawhalo", 128, 3 * 2 * NCH * 3, BF16)
    WSW = 69 * 128
    wsl = [sb("ws0", 128, WSW, BF16), sb("ws1", 128, WSW, BF16)]
    w2sl = [sb("w2s0", 128, 32 * 128, BF16), sb("w2s1", 128, 32 * 128, BF16)]
    x_bf = sb("x_bf", 128, NCH * STW, BF16)
    v_in = sb("v_in", 128, NCH * 2 * (63 + TT), BF16)
    shareA = sb("shareA", 128, 2 * NCH * STW, BF16)
    o_sl = sb("o_sl", 128, NCH * STW, BF16)
    rawg1 = sb("rawg1", 128, NCH * 2 * 259, BF16)
    mid_bf = sb("mid_bf", 128, NCH * STW, BF16)
    sqs = sb("sqs", 128, 4 * 512, BF16)
    spool = sb("spool", 128, NCH * 512, F32)
    xf32 = sb("xf32", 128, 2 * 512, F32)
    outstg = sb("outstg", 128, 2 * 512, F32)
    cbbc = sb("cbbc", 128, 2 * 512, BF16)
    rowsAf = sb("rowsAf", 4, 2048, F32)
    rowsAb = sb("rowsAb", 4, 4 * 512, BF16)
    rowsFb = sb("rowsFb", 4, 5 * 512, BF16)
    rows16 = sb("rows16", 16, 512, BF16)
    rows8b = sb("rows8b", 8, 512, BF16)
    rows4f = sb("rows4f", 4, 512, F32)
    pmscr = sb("pmscr", 128, 16, F32); pmscr2 = sb("pmscr2", 128, 16, F32)
    pmscr3 = sb("pmscr3", 128, 16, F32)
    pmbf = sb("pmbf", 128, 16, BF16)
    glA = sb("glA", 128, 32, F32); glB = sb("glB", 128, 32, F32)
    glC = sb("glC", 128, 32, F32); glD = sb("glD", 128, 32, F32)
    glbf = sb("glbf", 128, 32, BF16)
    gm0 = sb("gm0", 128, 4, F32); gm1 = sb("gm1", 128, 4, F32)
    gg0 = sb("gg0", 128, 4, F32); gg1 = sb("gg1", 128, 4, F32)
    dX = [[sb(f"dX{g}{p}", 128, 128, BF16) for p in range(2)] for g in range(3)]
    dU = [[sb(f"dU{g}{p}", 128, 128, BF16) for p in range(2)] for g in range(3)]
    dY = [[sb(f"dY{g}{p}", 128, 256, BF16) for p in range(2)] for g in range(3)]
    dAT = [sb(f"dAT{g}", 128, 128, BF16) for g in range(3)]
    dKB = [sb(f"dKB{g}", 128, 256, BF16) for g in range(3)]
    dVH = [sb(f"dVH{g}", 128, 256, BF16) for g in range(3)]
    dKP = [sb(f"dKP{g}", 128, 256, BF16) for g in range(3)]

    PB = [nc.alloc_psum_tensor(f"pb{i}", [128, 512], F32) for i in range(8)]

    OPS = {"tensor": [], "vector": [], "scalar": [], "gpsimd": [], "sync": []}
    SEMS = {n: [0] for n in ["pe", "dve", "act", "ldg", "ldw2", "ldw3", "wld", "ios"]}
    LABELS = {k: [] for k in OPS}
    PH = ["init"]

    def op(eng, method, *args, wait=None, inc=None, **kw):
        OPS[eng].append((method, args, kw, list(wait or []), inc))
        LABELS[eng].append(PH[0])
        if inc is not None:
            SEMS[inc[0]][0] += inc[1]

    def cur(s):
        return SEMS[s][0]

    BK = [None] * 8
    PEG = [0] * 8
    ready = {}

    def mark(key, sem):
        ready[key] = (sem, cur(sem))

    def uses(*keys):
        w = {}
        for k in keys:
            if k is None:
                continue
            s, c = k if isinstance(k, tuple) else ready.get(k, (None, 0))
            if s is None or c <= 0:
                continue
            if c > w.get(s, -1):
                w[s] = c
        return [(s, c) for s, c in w.items()]

    def epoch(bank, items, deps=()):
        w0 = uses(*deps, *(items[0][3]), BK[bank])
        n = len(items)
        for i, (o, l, r, dk) in enumerate(items):
            ww = w0 if i == 0 else (uses(*dk) if dk else None)
            op("tensor", "matmul", o, l, r, start=(i == 0), stop=(i == n - 1),
               wait=ww, inc=(("pe", 1) if i == n - 1 else None))
        PEG[bank] = cur("pe")
        BK[bank] = None

    def mm_group(bank, mms, deps=()):
        epoch(bank, [(o, l, r, []) for (o, l, r) in mms], deps=deps)

    def evac(eng, bank, method, *args, deps=(), key=None, **kw):
        sem = "dve" if eng == "vector" else "act"
        op(eng, method, *args, wait=uses(("pe", PEG[bank]), *deps), inc=(sem, 1), **kw)
        BK[bank] = (sem, cur(sem))
        if key:
            mark(key, sem)

    def veop(eng, method, *args, deps=(), key=None, **kw):
        sem = "dve" if eng == "vector" else "act"
        op(eng, method, *args, wait=uses(*deps), inc=(sem, 1), **kw)
        if key:
            mark(key, sem)

    def dma(queue, dst, src, key=None, deps=(), sem=None, **kw):
        if sem is None:
            sem = "ldg" if queue == "gpsimd" else "ios"
        op(queue, "dma_start", dst, src, wait=uses(*deps), inc=(sem, 16), **kw)
        if key:
            mark(key, sem)

    def xbf(c, lo=0, w=STW):
        return x_bf[:, c * STW + lo: c * STW + lo + w]

    def vin(c, s, lo, w):
        b = c * (2 * (63 + TT)) + s * (63 + TT) + 63 + lo
        return v_in[:, b: b + w]

    def vinr(c, s, lo, w):
        b = c * (2 * (63 + TT)) + s * (63 + TT) + lo
        return v_in[:, b: b + w]

    def vin_st(c):
        b = c * (2 * (63 + TT))
        return v_in[:, b: b + 2 * (63 + TT)].rearrange(
            "p (s t) -> p s t", s=2)[:, :, 63:63 + TT]

    def vin2(c, lo):
        b = c * (2 * (63 + TT))
        return v_in[:, b: b + 2 * (63 + TT)].rearrange(
            "p (s t) -> p s t", s=2)[:, :, lo:lo + TT]

    def shA(idx, c, lo=0, w=STW):
        b = idx * NCH * STW + c * STW + lo
        return shareA[:, b: b + w]

    def osl(c, lo=0, w=STW):
        return o_sl[:, c * STW + lo: c * STW + lo + w]

    def midv(c, lo=0, w=STW):
        return mid_bf[:, c * STW + lo: c * STW + lo + w]

    def hview(fc, lo=0, w=STW):
        if fc < 16:
            return shareA[:, fc * STW + lo: fc * STW + lo + w]
        if fc < 24:
            return o_sl[:, (fc - 16) * STW + lo: (fc - 16) * STW + lo + w]
        return x_bf[:, (fc - 24) * STW + lo: (fc - 24) * STW + lo + w]

    def raww(c, s, lo, w):
        b = c * 518 + s * 259 + lo
        return rawg1[:, b: b + w]

    def raw2(c, lo):
        b = c * 518
        return rawg1[:, b: b + 518].rearrange(
            "p (s t) -> p s t", s=2)[:, :, lo:lo + TT]

    def g1v(m, lo=0, w=512):
        return rawg1[:, m * 512 + lo: m * 512 + lo + w]

    def sel(j):
        return sumsel[:, j * 68:(j + 1) * 68]

    def rs4(h):
        return rowsel4[:, h * 128:(h + 1) * 128]

    def rs8(e):
        return rowsel8[:, e * 128:(e + 1) * 128]

    def rAb(i, lo=0, w=512):
        return rowsAb[:, i * 512 + lo: i * 512 + lo + w]

    def init_phase():
        loads = [
            (ident[:, :], ident_i), (masksl[:, :], masksl_i),
            (masksu[:, :], masksu_i), (maskui[:, :], maskui_i),
            (ones1f[:, :], ones1_i), (onescol[:, :], onescol_i),
            (statc4[:, :], statc4_i), (statc2[:, :], statc2_i), (epsc[:, :], epsc_i),
            (wb_s.rearrange("p (c h) -> p c h", h=4), wb.rearrange("(c p) h -> p c h", p=128)),
            (wres_s.rearrange("p (c h) -> p c h", h=4), wres.rearrange("(c p) h -> p c h", p=128)),
            (wg2_s.rearrange("p (c h) -> p c h", h=4), wg2.rearrange("(c p) h -> p c h", p=128)),
            (gwT_s.rearrange("p (c h) -> p c h", h=8), gwT.rearrange("(c p) h -> p c h", p=128)),
            (wog2_s[:, :], wog2),
            (wg1b_s[:, :], wg1b), (wog1b_s[:, :], wog1b),
            (bg1_s[:, :], bg1), (bog1_s[:, :], bog1), (bg2r_s[:, :], bg2r),
            (nw_s[:, :], nw), (bres_s[:, :], bres_i), (bog2_s[:, :], bog2_i),
            (crl_s[:, :], crl_i), (ltemp_s[:, :], ltemp_i),
        ]
        for j in range(12):
            loads.append((sel(j), sumsel_i[j]))
        for h in range(4):
            loads.append((rs4(h), rowsel4_i[h]))
        for e in range(8):
            loads.append((rs8(e), rowsel8_i[e]))
        for d, s in loads:
            dma("gpsimd", d, s)
        mark("init", "ldg")
        veop("vector", "memset", S_f[:, :], 0.0, key="S_f")
        veop("vector", "memset", S_b[:, :], 0.0, key="S_b")
        veop("vector", "memset", rawhalo[:, :], 0.0, key="halo")
        veop("vector", "memset", v_in[:, :], 0.0, key="v")
        veop("scalar", "activation", static_s[:, :], crl_s[:, :], AF.Sigmoid,
             deps=["init"], key="static")
        veop("scalar", "activation", tmp11[:, :], ltemp_s[:, :], AF.Abs, deps=["init"])
        veop("scalar", "activation", tmp11b[:, :], tmp11[:, :], AF.Exp, scale=-1.0)
        veop("scalar", "activation", tmp11c[:, :], tmp11b[:, :], AF.Ln, bias=1.0)
        veop("scalar", "activation", tmp11d[:, :], ltemp_s[:, :], AF.Relu, deps=["init"])
        veop("vector", "tensor_add", tmp11[:, :], tmp11c[:, :], tmp11d[:, :],
             deps=[("act", cur("act"))])
        veop("vector", "reciprocal", tinv_s[:, :], tmp11[:, :],
             deps=[("dve", cur("dve"))], key="tinv")
        mm_group(7, [(PB[7][0:128, 0:1], ones1f[:, :], tinv_s[:, :])],
                 deps=["tinv", "init"])
        evac("vector", 7, "tensor_copy", tinv_bc[:, :], PB[7][0:128, 0:1], key="tinvbc")
        itemsA = [
            ("scwq", sc_attn[0], wq.rearrange("(kc p) m -> p kc m", p=128)),
            ("sccdq", sc_cdqkv[0], cdq.rearrange("c t p q -> p c t q")),
            ("scwk", sc_attn[1], wk.rearrange("(kc p) m -> p kc m", p=128)),
            ("sccdk", sc_cdqkv[1], cdk.rearrange("c t p q -> p c t q")),
            ("scwv", sc_attn[2], wv.rearrange("(kc p) m -> p kc m", p=128)),
            ("sccdv", sc_cdqkv[2], cdv.rearrange("c t p q -> p c t q")),
        ]
        for c in range(NCH):
            itemsA.append((f"sccdl{c}", sc_cdl[c], cdl[c].rearrange("t p q -> p t q")))
            itemsA.append((f"sccds{c}", sc_cds[c], cds[c].rearrange("t p q -> p t q")))
        itemsA += [
            ("scwg1", sc_attn[3], wg1.rearrange("(kc p) m -> p kc m", p=128)),
            ("scwog1", sc_wog1, wog1.rearrange("(kc p) m -> p kc m", p=128)),
            ("scwo", sc_attn[4], wo.rearrange("(kc p) m -> p kc m", p=128)),
        ]
        for i, (nm, dst, srcap) in enumerate(itemsA):
            si = i % 2
            shp = list(srcap.shape)
            ncols = 1
            for dd in shp[1:]:
                ncols *= dd
            slotv = wsl[si][:, 0:ncols]
            if len(shp) == 3:
                slotv = slotv.rearrange("p (a b) -> p a b", a=shp[1])
            elif len(shp) == 4:
                slotv = slotv.rearrange("p (a b c) -> p a b c", a=shp[1], b=shp[2])
            dma("gpsimd", slotv, srcap, deps=[("ldg", cur("ldg"))])
            dma("gpsimd", dst, wsl[si][:, 0:ncols], deps=[("ldg", cur("ldg"))])
            mark(nm, "ldg")
        mark("convA", "ldg")
        for e in range(EXP):
            for fs in range(4):
                for hh in range(2):
                    srcap = w1T[e][hh * 512:(hh + 1) * 512,
                                   fs * 1024:(fs + 1) * 1024].rearrange(
                                       "(kc p) f -> p kc f", p=128)
                    dma("gpsimd", w2sl[0][:, 0:4096].rearrange(
                            "p (a b) -> p a b", a=4),
                        srcap, deps=[("ldw2", cur("ldw2"))], sem="ldw2")
                    dma("gpsimd", sc_w1[e * 4 + fs][:, hh * 4096:(hh + 1) * 4096],
                        w2sl[0][:, 0:4096], deps=[("ldw2", cur("ldw2"))], sem="ldw2")
                mark(f"scw1_{e}_{fs}", "ldw2")
        for e in range(EXP):
            for do in range(NCH):
                dma("gpsimd", w2sl[1][:, 0:4096].rearrange(
                        "p (a b) -> p a b", a=32),
                    w2r[e, do].rearrange("(kc p) m -> p kc m", p=128),
                    deps=[("ldw3", cur("ldw3"))], sem="ldw3")
                dma("gpsimd", sc_w2[e * 8 + do], w2sl[1][:, 0:4096],
                    deps=[("ldw3", cur("ldw3"))], sem="ldw3")
                mark(f"scw2_{e}_{do}", "ldw3")
        mark("convB", "ldw2")
        mark("convB2", "ldw3")

    slot_last = {0: None, 1: None, 10: None, 11: None}

    def tap(name, st, ap):
        if name not in tapset or st != 0:
            return
        shp = list(ap.shape)
        d = nc.dram_tensor(f"tap_{name}", shp, F32, kind="ExternalOutput").ap()
        tap_drams[name] = d
        dma("gpsimd", d, ap,
            deps=[("pe", cur("pe")), ("dve", cur("dve")), ("act", cur("act")),
                  ("ios", cur("ios"))])

    def phase_proj(st):
        t0 = st * TT
        for c in range(NCH):
            for s in range(2):
                veop("vector", "tensor_copy", vinr(c, s, 0, 63), vin(c, s, TT - 63, 63),
                     deps=(["v", "v_dead_pe"] if st > 0 else ["v"]))
        mark("vrot", "dve")
        for c in range(NCH):
            dma("gpsimd", xbf(c).rearrange("p (s t) -> p s t", s=2),
                x_t[:, c * 128:(c + 1) * 128, t0:t0 + TT].rearrange("s p t -> p s t"),
                sem="wld", deps=(["xbf_used"] if c == 0 else []))
        mark("x", "wld")

        projs = [("q", 0, 0), ("k", 1, 1), ("v", 2, 2)]
        for (pn, wi, pidx) in projs:
            dma("gpsimd", wsl[0][:, 0:8192], sc_attn[wi], key="wP", sem="wld",
                deps=["convA", "scw" + pn] + ([slot_last[0]] if slot_last[0] else []))
            dma("gpsimd", wsl[1][:, 0:4 * NCH * 128], sc_cdqkv[pidx], key="cdP",
                sem="wld",
                deps=["convA", "sccd" + pn] + ([slot_last[1]] if slot_last[1] else []))
            for m in range(NCH):
                b = m % 2
                mm_group(b, [(PB[b][:, 0:512],
                              wsl[0][:, k * 1024 + m * 128: k * 1024 + m * 128 + 128],
                              xbf(k)) for k in range(NCH)],
                         deps=["wP", "x"])
                for s in range(2):
                    evac("scalar", b, "tensor_copy", raww(m, s, 3, 256),
                         PB[b][:, s * 256: s * 256 + 256], key=f"raw{m}")
            slot_last[0] = ("pe", cur("pe"))
            for c in range(NCH):
                for s in range(2):
                    hb = pidx * (2 * NCH * 3) + s * (NCH * 3) + c * 3
                    veop("vector", "tensor_copy", raww(c, s, 0, 3),
                         rawhalo[:, hb:hb + 3],
                         deps=["halo", f"raw{c}", ("act", cur("act"))])
                    veop("vector", "tensor_copy", rawhalo[:, hb:hb + 3],
                         raww(c, s, 256, 3), deps=[f"raw{c}"])
            mark("halo", "dve")
            mark("rawfill", "dve")
            for c in range(NCH):
                b = 2 + (c % 2)
                mm_group(b, [(PB[b][:, 0:512],
                              wsl[1][:, (c * 4 + t) * 128:(c * 4 + t) * 128 + 128],
                              raw2(c, t)) for t in range(4)],
                         deps=["cdP", f"raw{c}", "rawfill"])
                if pn == "v":
                    for s in range(2):
                        evac("scalar", b, "activation", vin(c, s, 0, 256),
                             PB[b][:, s * 256:s * 256 + 256], AF.Silu,
                             deps=["vrot", "v_dead_dve"], key="v")
                else:
                    idx = 0 if pn == "q" else 1
                    evac("scalar", b, "activation", shA(idx, c),
                         PB[b][:, 0:512], AF.Silu,
                         deps=["ls_dead", "ll_dead"], key=("qs" if pn == "q" else "ks"))
            slot_last[1] = ("pe", cur("pe"))
        for idx, nm, okey in ((0, "qs", "qn"), (1, "ks", "kn")):
            items = []
            pe_ck = {}
            for c in range(NCH):
                ping = c % 2
                sq_t = sqs[:, ping * 512:(ping + 1) * 512]
                veop("scalar", "activation", sq_t, shA(idx, c), AF.Square,
                     deps=[nm, ("pe", pe_ck.get(c - 2, cur("pe") if c < 2 else 0))],
                     key=f"sq{c}")
                items.append((PB[4][0:68, :], sel(c // 2), sq_t, [f"sq{c}"]))
            w0 = uses(*(items[0][3]), BK[4])
            for i, (o, l, r, dk) in enumerate(items):
                op("tensor", "matmul", o, l, r, start=(i == 0),
                   stop=(i == len(items) - 1),
                   wait=(w0 if i == 0 else uses(*dk)),
                   inc=("pe", 1))
                pe_ck[i] = cur("pe")
            PEG[4] = cur("pe"); BK[4] = None
            evac("scalar", 4, "activation", rows4f[:, 0:512], PB[4][0:4, :], AF.Sqrt,
                 bias=epsc[0:4, 0:1], deps=["init"], key="l2a")
            veop("vector", "reciprocal", rAb(3), rows4f[:, 0:512],
                 deps=["l2a"], key="l2r")
            for h in range(4):
                mm_group(5, [(PB[5][:, 0:512], rs4(h), rAb(3))],
                         deps=["l2r"])
                for cc in (2 * h, 2 * h + 1):
                    evac("vector", 5, "tensor_mul", shA(idx, cc), shA(idx, cc),
                         PB[5][:, 0:512], deps=[nm], key=okey)
        mms = [(PB[6][0:4, :], wb_s[:, k * 4:(k + 1) * 4], xbf(k)) for k in range(NCH)]
        mms += [(PB[6][32:36, :], wres_s[:, k * 4:(k + 1) * 4], xbf(k)) for k in range(NCH)]
        mm_group(6, mms, deps=["x", "init"])
        evac("scalar", 6, "activation", rAb(0), PB[6][0:4, :], AF.Sigmoid, key="beta")
        tap("xbf", st, x_bf[:, :])
        tap("qn", st, shareA[:, 0:NCH * STW])
        tap("kn", st, shareA[:, NCH * STW:2 * NCH * STW])
        tap("vin", st, v_in[:, :])
        evac("scalar", 6, "activation", rAb(2), PB[6][32:36, :], AF.Sigmoid,
             bias=bres_s[:, 0:1], key="rg")
        veop("vector", "tensor_scalar", rAb(2), rAb(2), static_s[:, 0:1], None,
             ALU.mult, deps=["rg", "static"], key="rg")
        tap("beta", st, rowsAb[0:4, 0:512])
        tap("rg", st, rowsAb[0:4, 1024:1536])

    def delta_chunk(st, s, h, i):
        g = (s * 4 + h) % 3
        bP, bQ = 2 * g, 2 * g + 1
        oS = (s * 4 + h) * 512
        pc = s * 256 + i * 128

        def knc(k):
            return shA(1, 2 * h + k, pc, 128)

        def qnc(k):
            return shA(0, 2 * h + k, pc, 128)

        def Sb(k, lo=0, w=256):
            return S_b[:, oS + k * 256 + lo: oS + k * 256 + lo + w]

        mm_group(bP, [(PB[bP][:, 0:128], rs4(h), rAb(0, pc, 128))], deps=["beta"])
        for k in range(2):
            evac("vector", bP, "scalar_tensor_tensor", dKB[g][:, k * 128:(k + 1) * 128],
                 knc(k), -1.0, PB[bP][:, 0:128], ALU.mult, ALU.mult,
                 deps=["kn"], key=f"kb{g}")
            evac("vector", bP, "tensor_mul", dVH[g][:, k * 128:(k + 1) * 128],
                 vin(2 * h + k, s, i * 128, 128), PB[bP][:, 0:128],
                 deps=["v"], key=f"vh{g}")
        mm_group(bP, [(PB[bP][:, 0:128], dKB[g][:, k * 128:(k + 1) * 128], knc(k))
                      for k in range(2)] +
                     [(PB[bP][:, 128:256], knc(k), dKB[g][:, k * 128:(k + 1) * 128])
                      for k in range(2)],
                 deps=[f"kb{g}", "kn"])
        evac("vector", bP, "tensor_mul", dX[g][0][:, :], PB[bP][:, 0:128],
             masksl[:, :], key=f"X{g}")
        op("vector", "tensor_mul", dU[g][0][:, :], PB[bP][:, 128:256], masksu[:, :],
           wait=uses(("pe", PEG[bP])), inc=("dve", 1))
        mark(f"U{g}", "dve")
        BK[bP] = ("dve", cur("dve"))
        mm_group(bQ, [(PB[bQ][:, k * 128:(k + 1) * 128],
                       dVH[g][:, k * 128:(k + 1) * 128], ident[:, :])
                      for k in range(2)] +
                     [(PB[bQ][:, 0:256], dKB[g][:, k * 128:(k + 1) * 128], Sb(k))
                      for k in range(2)],
                 deps=[f"kb{g}", f"vh{g}", "S_b"])
        evac("scalar", bQ, "tensor_copy", dY[g][0][:, :], PB[bQ][:, 0:256], key=f"y{g}")
        for j in range(7):
            pj = j % 2
            mm_group(bP, [(PB[bP][:, 0:256], dU[g][pj][:, :], dY[g][pj][:, :]),
                          (PB[bP][:, 0:256], ident[:, :], dY[g][pj][:, :])],
                     deps=[f"U{g}", f"y{g}"])
            evac("scalar", bP, "tensor_copy", dY[g][1 - pj][:, :], PB[bP][:, 0:256],
                 key=f"y{g}")
            if j < 6:
                mm_group(bQ, [(PB[bQ][:, 0:128], dU[g][pj][:, :], dX[g][pj][:, :]),
                              (PB[bQ][:, 128:256], dX[g][pj][:, :], dU[g][pj][:, :])],
                         deps=[f"X{g}", f"U{g}"])
                evac("vector", bQ, "tensor_copy", dX[g][1 - pj][:, :], PB[bQ][:, 0:128],
                     key=f"X{g}")
                op("scalar", "tensor_copy", dU[g][1 - pj][:, :], PB[bQ][:, 128:256],
                   wait=uses(("pe", PEG[bQ])), inc=("act", 1))
                mark(f"U{g}", "act")
                BK[bQ] = ("act", cur("act"))
        ui2 = dY[g][1]
        mm_group(bP, [(PB[bP][:, 0:128], knc(k), qnc(k)) for k in range(2)],
                 deps=["qn", "kn"])
        evac("vector", bP, "tensor_mul", dAT[g][:, :], PB[bP][:, 0:128],
             maskui[:, :], key=f"at{g}")
        mms = []
        for m in range(2):
            mms.append((PB[bQ][:, m * 128:(m + 1) * 128], Sb(0, m * 128, 128), qnc(0)))
            mms.append((PB[bQ][:, m * 128:(m + 1) * 128], Sb(1, m * 128, 128), qnc(1)))
            mms.append((PB[bQ][:, m * 128:(m + 1) * 128],
                        ui2[:, m * 128:(m + 1) * 128], dAT[g][:, :]))
        mm_group(bQ, mms, deps=[f"at{g}", f"y{g}", "S_b", "qn"])
        for m in range(2):
            evac("scalar", bQ, "tensor_copy", osl(2 * h + m, pc, 128),
                 PB[bQ][:, m * 128:(m + 1) * 128], key="delta")
        mm_group(bP, [(PB[bP][:, k * 128:(k + 1) * 128], knc(k), ident[:, :])
                      for k in range(2)], deps=["kn"])
        evac("scalar", bP, "tensor_copy", dKP[g][:, :], PB[bP][:, 0:256], key=f"kp{g}")
        mm_group(bQ, [(PB[bQ][:, k * 256:(k + 1) * 256],
                       dKP[g][:, k * 128:(k + 1) * 128], ui2[:, :]) for k in range(2)],
                 deps=[f"kp{g}", f"y{g}"])
        for k in range(2):
            evac("vector", bQ, "tensor_add", S_f[:, oS + k * 256: oS + k * 256 + 256],
                 S_f[:, oS + k * 256: oS + k * 256 + 256],
                 PB[bQ][:, k * 256:(k + 1) * 256], deps=["S_f", "S_b"], key="S_f")
        for k in range(2):
            veop("scalar", "tensor_copy", Sb(k), S_f[:, oS + k * 256: oS + k * 256 + 256],
                 deps=["S_f"], key="S_b")

    def phase_delta(st):
        for i in range(TT // CH):
            for s in range(2):
                for h in range(4):
                    delta_chunk(st, s, h, i)
        mark("qn_dead", "pe")
        mark("kn_dead", "pe")
        tap("delta", st, o_sl[:, :])
        tap("S", st, S_f[:, :])

    def phase_fir(st):
        for c in range(NCH):
            si = c % 2
            dma("gpsimd", wsl[si][:, 0:64 * 128], sc_cdl[c],
                key=f"fir{c}", sem="wld",
                deps=["convA", f"sccdl{c}"] + ([slot_last[si]] if slot_last[si] else []))
            dma("gpsimd", wsl[si][:, 64 * 128:69 * 128], sc_cds[c], key=f"firs{c}",
                sem="wld",
                deps=["convA", f"sccds{c}"] + ([slot_last[si]] if slot_last[si] else []))
            b = c % 2
            mm_group(b, [(PB[b][:, 0:512], wsl[si][:, t * 128:(t + 1) * 128],
                          vin2(c, t)) for t in range(64)],
                     deps=[f"fir{c}", "v", "vrot"])
            evac("scalar", b, "tensor_copy", shA(1, c), PB[b][:, 0:512],
                 deps=["kn_dead"], key="ll")
            b2 = 2 + (c % 2)
            mm_group(b2, [(PB[b2][:, 0:512],
                           wsl[si][:, (64 + t) * 128:(64 + t + 1) * 128],
                           vin2(c, 59 + t)) for t in range(5)],
                      deps=[f"firs{c}", "v", "vrot"])
            evac("scalar", b2, "tensor_copy", shA(0, c), PB[b2][:, 0:512],
                 deps=["qn_dead"], key="ls")
            slot_last[si] = ("pe", cur("pe"))
            slot_last[10 + (c % 2)] = ("pe", cur("pe"))
        tap("ls", st, shareA[:, 0:NCH * STW])
        tap("ll", st, shareA[:, NCH * STW:2 * NCH * STW])

    def stats_emit(branches, coef, width, outbank, outrows, outkey, outbuf):
        first_epoch = True
        nb = len(branches)
        for b, (bview, bkeys) in enumerate(branches):
            items = []
            pe_ck = {}
            for c in range(NCH):
                ping = c % 2
                sq_t = sqs[:, ping * 1024: ping * 1024 + 512]
                ab_t = sqs[:, ping * 1024 + 512: ping * 1024 + 1024]
                scrdep = ("pe", pe_ck.get(3 * (c - 2) + 2, cur("pe") if c < 2 else 0))
                veop("scalar", "activation", sq_t, bview(c), AF.Square,
                     deps=list(bkeys) + [scrdep], key=f"stsq{c}")
                veop("scalar", "activation", ab_t, bview(c), AF.Abs,
                     deps=list(bkeys), key=f"stab{c}")
                items.append((PB[6][0:68, :], sel(c // 2), bview(c), list(bkeys)))
                items.append((PB[6][0:68, :], sel(4 + c // 2), sq_t, [f"stsq{c}"]))
                items.append((PB[6][0:68, :], sel(8 + c // 2), ab_t, [f"stab{c}"]))
            w0 = uses(*(items[0][3]), BK[6])
            for i, (o, l, r, dk) in enumerate(items):
                op("tensor", "matmul", o, l, r, start=(i == 0),
                   stop=(i == len(items) - 1),
                   wait=(w0 if i == 0 else uses(*dk)), inc=("pe", 1))
                pe_ck[i] = cur("pe")
            PEG[6] = cur("pe"); BK[6] = None
            evac("scalar", 6, "tensor_copy", rowsFb[:, 0:512], PB[6][0:4, :], key="fA")
            op("scalar", "activation", rowsFb[:, 512:1024], PB[6][0:4, :], AF.Square,
               wait=uses(("pe", PEG[6])), inc=("act", 1)); mark("fsA", "act")
            op("scalar", "tensor_copy", rowsFb[:, 1024:1536], PB[6][32:36, :],
               wait=uses(("pe", PEG[6])), inc=("act", 1)); mark("fB", "act")
            op("scalar", "activation", rowsFb[:, 1536:2048], PB[6][32:36, :], AF.Sqrt,
               wait=uses(("pe", PEG[6])), inc=("act", 1)); mark("fsB", "act")
            op("scalar", "tensor_copy", rowsFb[:, 2048:2560], PB[6][64:68, :],
               wait=uses(("pe", PEG[6])), inc=("act", 1)); mark("fC", "act")
            BK[6] = ("act", cur("act"))
            cof = lambda w: coef[:, (b * 5 + w) * width:(b * 5 + w + 1) * width]
            items2 = [(cof(0), rowsFb[:, 0:512], ["fA", "init"]),
                      (cof(1), rowsFb[:, 512:1024], ["fsA"]),
                      (cof(2), rowsFb[:, 1024:1536], ["fB"]),
                      (cof(3), rowsFb[:, 1536:2048], ["fsB"]),
                      (cof(4), rowsFb[:, 2048:2560], ["fC"])]
            for i, (l, r, dk) in enumerate(items2):
                ww = uses(*dk, BK[outbank]) if (first_epoch and i == 0) else uses(*dk)
                st_ = first_epoch and i == 0
                sp_ = (b == nb - 1) and (i == len(items2) - 1)
                op("tensor", "matmul", PB[outbank][0:outrows, :], l, r,
                   start=st_, stop=sp_, wait=ww, inc=("pe", 1))
            first_epoch = False
        PEG[outbank] = cur("pe"); BK[outbank] = None
        evac("scalar", outbank, "tensor_copy", outbuf, PB[outbank][0:outrows, :],
             key=outkey)

    def phase_stats(st):
        branches = [(lambda c: shA(0, c), ["ls"]),
                    (lambda c: shA(1, c), ["ll"]),
                    (lambda c: osl(c), ["delta"]),
                    (lambda c: vin_st(c), ["v"])]
        stats_emit(branches, statc4, 16, 7, 16, "st16", rows16[:, :])
        tap("st16", st, rows16[:, :])

    def phase_gates(st):
        dma("gpsimd", wsl[0][:, 0:8192].rearrange("p (kc m) -> p kc m", kc=8),
            wg1.rearrange("(kc p) m -> p kc m", p=128), key="wg1l", deps=[slot_last[0]] if slot_last[0] else [])
        for m in range(NCH):
            b = m % 2
            mms = [(PB[b][:, 0:512],
                    wsl[0][:, k * 1024 + m * 128: k * 1024 + m * 128 + 128], xbf(k))
                   for k in range(NCH)]
            mms.append((PB[b][:, 0:512], wg1b_s[:, m * 128:(m + 1) * 128], rows16[:, :]))
            mm_group(b, mms, deps=["wg1l", "x", "st16"])
            evac("scalar", b, "activation", g1v(m), PB[b][:, 0:512], AF.Gelu,
                 bias=bg1_s[:, m:m + 1], key="g1")
        slot_last[0] = ("pe", cur("pe"))
        mms = []
        for tc in range(4):
            for k in range(NCH):
                mms.append((PB[2][:, tc * 4:(tc + 1) * 4],
                            g1v(k, tc * 128, 128), wg2_s[:, k * 4:(k + 1) * 4]))
        mm_group(2, mms, deps=["g1", "init"])
        mark("g1_dead", "pe")
        for tc in range(4):
            t4 = slice(tc * 4, (tc + 1) * 4)
            evac("vector", 2, "tensor_add", pmscr[:, t4], PB[2][:, t4], bg2r_s[:, :],
                 deps=["init"])
            veop("vector", "tensor_scalar", pmscr[:, t4], pmscr[:, t4],
                 tinv_bc[:, 0:1], None, ALU.mult, deps=["tinvbc"])
            veop("vector", "tensor_reduce", gm0[:, tc:tc + 1], pmscr[:, t4], AX,
                 ALU.max, deps=[("dve", cur("dve"))])
            veop("vector", "tensor_scalar", pmscr2[:, t4], pmscr[:, t4],
                 gm0[:, tc:tc + 1], None, ALU.subtract, deps=[("dve", cur("dve"))])
            veop("scalar", "activation", pmscr3[:, t4], pmscr2[:, t4], AF.Exp,
                 deps=[("dve", cur("dve"))])
            veop("vector", "tensor_reduce", gm1[:, tc:tc + 1], pmscr3[:, t4], AX,
                 ALU.add, deps=[("act", cur("act"))])
            veop("vector", "reciprocal", gg0[:, tc:tc + 1], gm1[:, tc:tc + 1],
                 deps=[("dve", cur("dve"))])
            veop("vector", "tensor_scalar", pmscr3[:, t4], pmscr3[:, t4],
                 gg0[:, tc:tc + 1], None, ALU.mult, deps=[("dve", cur("dve"))])
            veop("vector", "tensor_scalar_max", pmscr3[:, t4], pmscr3[:, t4], 0.02)
            veop("vector", "tensor_reduce", gg1[:, tc:tc + 1], pmscr3[:, t4], AX,
                 ALU.add, deps=[("dve", cur("dve"))])
            veop("vector", "reciprocal", gm1[:, tc:tc + 1], gg1[:, tc:tc + 1],
                 deps=[("dve", cur("dve"))])
            veop("vector", "tensor_scalar", pmscr3[:, t4], pmscr3[:, t4],
                 gm1[:, tc:tc + 1], None, ALU.mult, deps=[("dve", cur("dve"))])
            veop("scalar", "tensor_copy", pmbf[:, t4], pmscr3[:, t4],
                 deps=[("dve", cur("dve"))], key="pmb")
        tap("pmscr", st, pmscr[:, :])
        tap("pmscr2", st, pmscr3[:, :])
        tap("gm1", st, gm1[:, :])
        tap("tinvbc", st, tinv_bc[:, :])
        mm_group(3, [(PB[3][0:4, tc * 128:(tc + 1) * 128], pmbf[:, tc * 4:(tc + 1) * 4],
                      ident[:, :]) for tc in range(4)], deps=["pmb", "init"])
        evac("scalar", 3, "tensor_copy", rAb(1), PB[3][0:4, 0:512], key="prow")
        for j, bk in ((1, 4), (2, 5), (3, 6)):
            mm_group(bk, [(PB[bk][:, 0:512], rs4(j), rAb(1))], deps=["prow"])
        t1 = sqs[:, 0:512]; t2 = sqs[:, 512:1024]; t3 = sqs[:, 1024:1536]
        for h in range(4):
            mm_group(3, [(PB[3][:, 0:512], rs4(0), rAb(1)),
                         (PB[3][:, 0:512], rs4(h), rAb(2))], deps=["prow", "rg"])
            for cc in (2 * h, 2 * h + 1):
                veop("vector", "tensor_mul", t1, shA(0, cc), PB[3][:, 0:512],
                     deps=["ls", ("pe", PEG[3]), ("pe", cur("pe"))])
                veop("vector", "tensor_mul", t2, shA(1, cc), PB[4][:, 0:512],
                     deps=["ll", ("pe", PEG[4])])
                veop("vector", "tensor_add", t1, t1, t2)
                veop("vector", "tensor_mul", t2, osl(cc), PB[5][:, 0:512],
                     deps=["delta", ("pe", PEG[5])])
                veop("vector", "tensor_mul", t3, vin_st(cc), PB[6][:, 0:512],
                     deps=["v", ("pe", PEG[6])])
                veop("vector", "tensor_add", t2, t2, t3)
                veop("vector", "tensor_add", osl(cc), t1, t2, key="omix")
            BK[3] = ("dve", cur("dve"))
        BK[4] = BK[5] = BK[6] = ("dve", cur("dve"))
        mark("ls_dead", "dve")
        mark("ll_dead", "dve")
        tap("g1", st, rawg1[:, 0:4096])
        tap("prow", st, rowsAb[0:4, 512:1024])
        tap("omix", st, o_sl[:, :])

    def phase_og(st):
        branches = [(lambda c: osl(c), ["omix"]), (lambda c: vin_st(c), ["v"])]
        stats_emit(branches, statc2, 8, 7, 8, "og8", rows8b[:, :])
        mark("v_dead_pe", "pe")
        mark("v_dead_dve", "dve")
        dma("gpsimd", wsl[1][:, 0:4096].rearrange("p (kc m) -> p kc m", kc=8),
            wog1.rearrange("(kc p) m -> p kc m", p=128), key="wog1l", deps=[slot_last[1]] if slot_last[1] else [])
        for m in range(4):
            b = m % 2
            mms = [(PB[b][:, 0:512],
                    wsl[1][:, k * 512 + m * 128: k * 512 + m * 128 + 128], xbf(k))
                   for k in range(NCH)]
            mms.append((PB[b][:, 0:512], wog1b_s[:, m * 128:(m + 1) * 128], rows8b[:, :]))
            mm_group(b, mms, deps=["wog1l", "x", "og8"])
            evac("scalar", b, "activation", g1v(m), PB[b][:, 0:512], AF.Gelu,
                 bias=bog1_s[:, m:m + 1], key="og1")
        slot_last[1] = ("pe", cur("pe"))
        mm_group(2, [(PB[2][0:1, 0:512], wog2_s[:, k:k + 1], g1v(k))
                     for k in range(4)], deps=["og1", "init"])
        mark("xbf_dead", "pe")
        evac("scalar", 2, "activation", rowsAf[0:1, 0:512], PB[2][0:1, 0:512],
             AF.Sigmoid, bias=bog2_s[0:1, 0:1], key="ogr")
        mm_group(3, [(PB[3][:, 0:512], ones1f[:, :], rowsAf[0:1, 0:512])], deps=["ogr"])
        for c in range(NCH):
            evac("vector", 3, "tensor_mul", osl(c), osl(c), PB[3][:, 0:512],
                 deps=["omix"], key="og_o")
        items = []
        pe_ck = {}
        for c in range(NCH):
            ping = c % 2
            sq_t = sqs[:, ping * 512:(ping + 1) * 512]
            veop("scalar", "activation", sq_t, osl(c), AF.Square,
                 deps=["og_o", ("pe", pe_ck.get(c - 2, cur("pe") if c < 2 else 0))],
                 key=f"rsq{c}")
            items.append((PB[4][0:68, :], sel(c // 2), sq_t, [f"rsq{c}"]))
        w0 = uses(*(items[0][3]), BK[4])
        for i, (o, l, r, dk) in enumerate(items):
            op("tensor", "matmul", o, l, r, start=(i == 0), stop=(i == len(items) - 1),
               wait=(w0 if i == 0 else uses(*dk)), inc=("pe", 1))
            pe_ck[i] = cur("pe")
        PEG[4] = cur("pe"); BK[4] = None
        evac("scalar", 4, "activation", rows4f[:, 0:512], PB[4][0:4, :], AF.Sqrt,
             scale=1.0 / 256.0, bias=epsc[0:4, 1:2], deps=["init"], key="rmsa")
        veop("vector", "reciprocal", rAb(3), rows4f[:, 0:512],
             deps=["rmsa"], key="rmsr")
        for h in range(4):
            mm_group(5, [(PB[5][:, 0:512], rs4(h), rAb(3))], deps=["rmsr"])
            for cc in (2 * h, 2 * h + 1):
                evac("vector", 5, "scalar_tensor_tensor", osl(cc), osl(cc),
                     nw_s[:, (cc % 2):(cc % 2) + 1], PB[5][:, 0:512],
                     ALU.mult, ALU.mult, deps=["og_o", "init"], key="on")
        tap("og8", st, rows8b[:, :])
        tap("on", st, o_sl[:, :])

    def ln_rows(srckey, srcbuf):
        items = []
        pe_ck = {}
        for c in range(NCH):
            ping = c % 2
            sb_t = sqs[:, ping * 1024: ping * 1024 + 512]
            sq_t = sqs[:, ping * 1024 + 512: ping * 1024 + 1024]
            scrdep = ("pe", pe_ck.get(2 * (c - 2) + 1, cur("pe") if c < 2 else 0))
            veop("scalar", "tensor_copy", sb_t, srcbuf[:, c * 512:(c + 1) * 512],
                 deps=[srckey, scrdep], key=f"lnc{c}")
            veop("scalar", "activation", sq_t, sb_t, AF.Square, key=f"lns{c}")
            items.append((PB[2][0:1, :], onescol[:, :], sb_t, [f"lnc{c}", "init"]))
            items.append((PB[2][32:33, :], onescol[:, :], sq_t, [f"lns{c}"]))
        w0 = uses(*(items[0][3]), BK[2])
        for i, (o, l, r, dk) in enumerate(items):
            op("tensor", "matmul", o, l, r, start=(i == 0), stop=(i == len(items) - 1),
               wait=(w0 if i == 0 else uses(*dk)), inc=("pe", 1))
            pe_ck[i] = cur("pe")
        PEG[2] = cur("pe"); BK[2] = None
        mean_r = rowsAf[0:1, 0:512]; msq_r = rowsAf[0:1, 512:1024]
        var_r = rowsAf[0:1, 1024:1536]; rstd_r = rowsAf[0:1, 1536:2048]
        m2_r = rows4f[0:1, 0:512]; sd_r = rowsAf[0:1, 512:1024]
        evac("vector", 2, "tensor_scalar", mean_r, PB[2][0:1, :], 1.0 / D, None,
             ALU.mult, key="lnm")
        op("vector", "tensor_scalar", msq_r, PB[2][32:33, :], 1.0 / D, None, ALU.mult,
           wait=uses(("pe", PEG[2])), inc=("dve", 1))
        BK[2] = ("dve", cur("dve"))
        veop("vector", "tensor_mul", m2_r, mean_r, mean_r,
             deps=[("dve", cur("dve"))])
        veop("vector", "tensor_sub", var_r, msq_r, m2_r,
             deps=[("dve", cur("dve"))])
        veop("scalar", "activation", sd_r, var_r, AF.Sqrt, bias=epsc[0:1, 1:2],
             deps=[("dve", cur("dve")), "init"])
        veop("vector", "reciprocal", rstd_r, sd_r,
             deps=[("act", cur("act")), ("dve", cur("dve"))], key="lnr")
        mm_group(3, [(PB[3][:, 0:512], ones1f[:, :], mean_r)], deps=["lnm", "lnr"])
        mm_group(4, [(PB[4][:, 0:512], ones1f[:, :], rstd_r)], deps=["lnr"])

    def phase_ln1(st):
        t0 = st * TT
        dma("gpsimd", wsl[0][:, 0:8192].rearrange("p (kc m) -> p kc m", kc=8),
            wo.rearrange("(kc p) m -> p kc m", p=128), key="wol", deps=[slot_last[0]] if slot_last[0] else [])
        for m in range(NCH):
            b = m % 2
            mm_group(b, [(PB[b][:, 0:512],
                          wsl[0][:, k * 1024 + m * 128: k * 1024 + m * 128 + 128],
                          osl(k)) for k in range(NCH)],
                     deps=["wol", "on"])
            ping = m % 2
            dma("sync", xf32[:, ping * 512:(ping + 1) * 512].rearrange(
                "p (s t) -> p s t", s=2),
                x_t[:, m * 128:(m + 1) * 128, t0:t0 + TT].rearrange("s p t -> p s t"),
                key=f"xf{m}", deps=([f"xfdead{ping}"] if m >= 2 else []))
            evac("vector", b, "tensor_add", spool[:, m * 512:(m + 1) * 512],
                 xf32[:, ping * 512:(ping + 1) * 512], PB[b][:, 0:512],
                 deps=[f"xf{m}"], key="ssum")
            mark(f"xfdead{ping}", "dve")
        slot_last[0] = ("pe", cur("pe"))
        mark("o_dead", "pe")
        tap("ssum", st, spool[:, :])
        ln_rows("ssum", spool)
        for c in range(NCH):
            veop("vector", "tensor_sub", xf32[:, 0:512],
                 spool[:, c * 512:(c + 1) * 512], PB[3][:, 0:512],
                 deps=["ssum", ("pe", PEG[3]), "xfdead0"])
            mark("xfdead0", "dve")
            veop("vector", "tensor_mul", midv(c), xf32[:, 0:512], PB[4][:, 0:512],
                 deps=[("pe", PEG[4]), "mid_dead"], key="mid")
        BK[3] = ("dve", cur("dve")); BK[4] = ("dve", cur("dve"))
        tap("mid", st, mid_bf[:, :])

    def phase_ffn(st):
        t0 = st * TT
        mms = []
        for tc in range(4):
            for k in range(NCH):
                mms.append((PB[5][:, tc * 8:(tc + 1) * 8],
                            midv(k, tc * 128, 128), gwT_s[:, k * 8:(k + 1) * 8]))
        mm_group(5, mms, deps=["mid", "init"])
        for tc in range(4):
            t8 = slice(tc * 8, (tc + 1) * 8)
            evac("vector", 5, "tensor_copy", glA[:, t8], PB[5][:, t8])
            veop("vector", "tensor_reduce", gm0[:, tc:tc + 1], glA[:, t8], AX,
                 ALU.max, deps=[("dve", cur("dve"))])
            veop("vector", "tensor_scalar", glB[:, t8], glA[:, t8],
                 gm0[:, tc:tc + 1], None, ALU.is_equal, deps=[("dve", cur("dve"))])
            veop("vector", "scalar_tensor_tensor", glC[:, t8], glB[:, t8], -1e30,
                 glA[:, t8], ALU.mult, ALU.add, deps=[("dve", cur("dve"))])
            veop("vector", "tensor_reduce", gm1[:, tc:tc + 1], glC[:, t8], AX,
                 ALU.max, deps=[("dve", cur("dve"))])
            veop("vector", "tensor_scalar", glD[:, t8], glC[:, t8],
                 gm1[:, tc:tc + 1], None, ALU.is_equal, deps=[("dve", cur("dve"))])
            veop("vector", "tensor_sub", gg0[:, tc:tc + 1], gm0[:, tc:tc + 1],
                 gm1[:, tc:tc + 1], deps=[("dve", cur("dve"))])
            veop("scalar", "activation", gg1[:, tc:tc + 1], gg0[:, tc:tc + 1],
                 AF.Sigmoid, deps=[("dve", cur("dve"))])
            veop("vector", "tensor_scalar", gg0[:, tc:tc + 1], gg1[:, tc:tc + 1],
                 -1.0, 1.0, ALU.mult, ALU.add, deps=[("act", cur("act"))])
            veop("vector", "tensor_scalar", glC[:, t8], glB[:, t8],
                 gg1[:, tc:tc + 1], None, ALU.mult, deps=[("act", cur("act")), ("dve", cur("dve"))])
            veop("vector", "scalar_tensor_tensor", glA[:, t8], glD[:, t8],
                 gg0[:, tc:tc + 1], glC[:, t8], ALU.mult, ALU.add,
                 deps=[("dve", cur("dve"))])
            veop("scalar", "tensor_copy", glbf[:, t8], glA[:, t8],
                 deps=[("dve", cur("dve"))], key="glb")
        mm_group(5, [(PB[5][0:8, tc * 128:(tc + 1) * 128], glbf[:, tc * 8:(tc + 1) * 8],
                      ident[:, :]) for tc in range(4)], deps=["glb", "init"])
        evac("scalar", 5, "tensor_copy", rows8b[:, :], PB[5][0:8, 0:512], key="cbrow")
        tap("cbrow", st, rows8b[:, :])
        for e in range(EXP):
            ping = e % 2
            mm_group(5, [(PB[5][:, 0:512], rs8(e), rows8b[:, :])], deps=["cbrow"])
            evac("vector", 5, "tensor_copy", cbbc[:, ping * 512:(ping + 1) * 512],
                 PB[5][:, 0:512], deps=[f"cbdead{ping}"], key=f"cb{e}")
            for fs in range(4):
                si = (e * 4 + fs) % 2
                dma("gpsimd", wsl[si][:, 0:8192], sc_w1[e * 4 + fs],
                    key=f"w1_{e}_{fs}", sem="wld",
                    deps=[f"scw1_{e}_{fs}"] + ([slot_last[si]] if slot_last[si] else []))
                for fl in range(8):
                    fc = fs * 8 + fl
                    b = fc % 2
                    mm_group(b, [(PB[b][:, 0:512],
                                  wsl[si][:, k * 1024 + fl * 128: k * 1024 + fl * 128 + 128],
                                  midv(k)) for k in range(NCH)],
                             deps=[f"w1_{e}_{fs}", "mid"])
                    evac("scalar", b, "activation", hview(fc), PB[b][:, 0:512],
                         AF.Relu, deps=["ls_dead", "ll_dead"], key="h")
                slot_last[si] = ("pe", cur("pe"))
            def w2load(do):
                wsi = 10 + (do % 2)
                dma("gpsimd", w2sl[do % 2][:, 0:4096], sc_w2[e * 8 + do],
                    key=f"w2_{e}_{do}", sem="wld",
                    deps=["convB", "convB2", f"scw2_{e}_{do}"] +
                         ([slot_last[wsi]] if slot_last[wsi] else []))
            w2load(0); w2load(1)
            for do in range(NCH):
                b = 2 + (do % 2)
                mm_group(b, [(PB[b][:, 0:512], w2sl[do % 2][:, k * 128:(k + 1) * 128],
                              hview(k)) for k in range(32)],
                         deps=[f"w2_{e}_{do}", "h"])
                slot_last[10 + (do % 2)] = ("pe", cur("pe"))
                if do + 2 < NCH:
                    w2load(do + 2)
                if e == 0:
                    evac("vector", b, "tensor_mul", spool[:, do * 512:(do + 1) * 512],
                         PB[b][:, 0:512], cbbc[:, ping * 512:(ping + 1) * 512],
                         deps=[f"cb{e}"], key="y")
                else:
                    evac("vector", b, "tensor_mul", xf32[:, 512:1024],
                         PB[b][:, 0:512], cbbc[:, ping * 512:(ping + 1) * 512],
                         deps=[f"cb{e}", "xfdead1"], key="yt")
                    veop("vector", "tensor_add", spool[:, do * 512:(do + 1) * 512],
                         spool[:, do * 512:(do + 1) * 512], xf32[:, 512:1024],
                         deps=["yt"], key="y")
                    mark("xfdead1", "dve")
            mark(f"cbdead{ping}", "pe")
        mark("mid_dead", "pe")
        mark("xbf_used", "pe")
        mark("shA_dead", "pe")
        tap("y", st, spool[:, :])
        for c in range(NCH):
            veop("vector", "tensor_add", spool[:, c * 512:(c + 1) * 512],
                 spool[:, c * 512:(c + 1) * 512], midv(c), deps=["y", "mid"], key="s2")
        ln_rows("s2", spool)
        for c in range(NCH):
            ping = c % 2
            veop("vector", "tensor_sub", xf32[:, 0:512],
                 spool[:, c * 512:(c + 1) * 512], PB[3][:, 0:512],
                 deps=["s2", ("pe", PEG[3]), "xfdead0"])
            mark("xfdead0", "dve")
            veop("vector", "tensor_mul", outstg[:, ping * 512:(ping + 1) * 512],
                 xf32[:, 0:512], PB[4][:, 0:512],
                 deps=[("pe", PEG[4]), f"stg{ping}"], key=f"out{c}")
            dma("sync", out_t[:, c * 128:(c + 1) * 128, t0:t0 + TT].rearrange(
                    "s p t -> p s t"),
                outstg[:, ping * 512:(ping + 1) * 512].rearrange(
                    "p (s t) -> p s t", s=2),
                deps=[f"out{c}"])
            mark(f"stg{ping}", "ios")
        BK[3] = ("dve", cur("dve")); BK[4] = ("dve", cur("dve"))
        mark("y_dead", "dve")

    init_phase()
    for st in range(NST):
        for phf in (phase_proj, phase_delta, phase_fir, phase_stats, phase_gates,
                    phase_og, phase_ln1, phase_ffn):
            PH[0] = phf.__name__.replace("phase_", "")
            phf(st)
    op("sync", "wait_ge_final")
    global _LAST
    _LAST = {"OPS": {k: [(m, kw) for (m, a, kw, w, i) in v] for k, v in OPS.items()},
             "LABELS": LABELS}

    import contextlib
    semobjs = {}
    with contextlib.ExitStack() as ctx:
        for n in SEMS:
            semobjs[n] = ctx.enter_context(nc.semaphore(n))
        block = ctx.enter_context(nc.Block())
        final_ios = cur("ios")

        def make_body(lst, isscalar=False):
            def body(e):
                for (method, args, kw, waits, inc) in lst:
                    if isscalar and method == "tensor_copy":
                        method = "copy"
                    if method == "wait_ge_final":
                        e.wait_ge(semobjs["ios"], final_ios)
                        continue
                    for (s, v) in waits:
                        e.wait_ge(semobjs[s], v)
                    instr = getattr(e, method)(*args, **kw)
                    if inc is not None:
                        instr.then_inc(semobjs[inc[0]], inc[1])
            return body

        with nc.allow_low_precision(reason="bf16 rstd broadcast rows; within tolerance"):
            for engname in ["gpsimd", "sync", "tensor", "scalar", "vector"]:
                getattr(block, engname)(make_body(OPS[engname], engname == "scalar"))
    return nc


_LAST = None


_CACHE = {}


def _consts():
    c = {}
    c["ident_i"] = np.eye(128, dtype=np.float32)
    c["masksl_i"] = np.tril(np.ones((128, 128), np.float32), -1)
    c["masksu_i"] = np.triu(np.ones((128, 128), np.float32), 1)
    c["maskui_i"] = np.triu(np.ones((128, 128), np.float32), 0)
    ss = np.zeros((12, 128, 68), np.float32)
    for g in range(3):
        for h in range(4):
            ss[g * 4 + h, :, (0, 32, 64)[g] + h] = 1.0
    c["sumsel_i"] = ss
    r4 = np.zeros((4, 4, 128), np.float32)
    for h in range(4):
        r4[h, h, :] = 1.0
    c["rowsel4_i"] = r4
    r8 = np.zeros((8, 8, 128), np.float32)
    for e in range(8):
        r8[e, e, :] = 1.0
    c["rowsel8_i"] = r8
    c["ones1_i"] = np.ones((1, 128), np.float32)
    c["onescol_i"] = np.ones((128, 1), np.float32)

    def statc(nb, width, dv, Hn):
        a = np.zeros((4, nb * 5 * width), np.float32)
        for b in range(nb):
            def tile(w):
                return a[:, (b * 5 + w) * width:(b * 5 + w + 1) * width]
            tile(0)[:, 4 * b + 0] = 1.0 / (dv * Hn)
            tile(1)[:, 4 * b + 1] = -1.0 / (dv * dv * Hn)
            tile(2)[:, 4 * b + 1] = 1.0 / (dv * Hn)
            tile(3)[:, 4 * b + 3] = 1.0 / Hn
            tile(4)[:, 4 * b + 2] = 1.0 / (dv * Hn)
        return a
    c["statc4_i"] = statc(4, 16, 256, 4)
    c["statc2_i"] = statc(2, 8, 256, 4)
    c["epsc_i"] = np.broadcast_to(np.array([1e-6, 1e-5], np.float32), (4, 2)).copy()
    return c


def _diag(w, ntap):
    z = np.zeros((8, ntap, 128, 128), np.float32)
    r = np.arange(128)
    wr = w.reshape(8, 128, ntap).transpose(0, 2, 1)
    z[:, :, r, r] = wr
    return z


def make_inputs(inputs, L, ncores):
    f = lambda k: np.asarray(inputs[k], np.float32)
    base = {}
    base["wq"] = f("L0_attn_Wq"); base["wk"] = f("L0_attn_Wk"); base["wv"] = f("L0_attn_Wv")
    base["wo"] = f("L0_attn_Wo")
    wg1 = f("L0_attn_Wg1")
    base["wg1"] = np.ascontiguousarray(wg1[:1024])
    base["wg1b"] = np.ascontiguousarray(wg1[1024:])
    wog1 = f("L0_attn_Wog1")
    base["wog1"] = np.ascontiguousarray(wog1[:1024])
    base["wog1b"] = np.ascontiguousarray(wog1[1024:])
    base["wb"] = f("L0_attn_Wb"); base["wres"] = f("L0_attn_Wres")
    base["wg2"] = f("L0_attn_Wg2")
    base["wog2"] = np.ascontiguousarray(f("L0_attn_Wog2").reshape(4, 128).T)
    base["gwT"] = np.ascontiguousarray(f("L0_ffn_gate_w").T)
    base["cdq"] = _diag(f("L0_attn_wq_conv"), 4)
    base["cdk"] = _diag(f("L0_attn_wk_conv"), 4)
    base["cdv"] = _diag(f("L0_attn_wv_conv"), 4)
    base["cds"] = _diag(f("L0_attn_fir_short").reshape(-1, 5), 5)
    base["cdl"] = _diag(f("L0_attn_fir_long").reshape(-1, 64), 64)
    w1 = f("L0_ffn_w1"); w2 = f("L0_ffn_w2")
    base["w1T"] = np.ascontiguousarray(w1.transpose(0, 2, 1))
    base["w2r"] = np.ascontiguousarray(
        w2.transpose(0, 2, 1).reshape(EXP, FF, 8, 128).transpose(0, 2, 1, 3))
    base["bg1"] = np.ascontiguousarray(f("L0_attn_bg1").reshape(8, 128).T)
    base["bog1"] = np.ascontiguousarray(f("L0_attn_bog1").reshape(4, 128).T)
    base["bg2r"] = np.ascontiguousarray(np.broadcast_to(f("L0_attn_bg2"), (128, 4)))
    base["nw"] = np.ascontiguousarray(f("L0_attn_norm_w").reshape(2, 128).T)
    base["bres_i"] = f("L0_attn_bres").reshape(4, 1)
    base["bog2_i"] = f("L0_attn_bog2").reshape(1, 1)
    base["crl_i"] = f("L0_attn_conv_res_logit").reshape(4, 1)
    base["ltemp_i"] = f("L0_attn_logit_temp").reshape(1, 1)
    base.update(_consts())
    x = f("L0_attn_hidden_states")
    in_maps = []
    for core in range(ncores):
        m = dict(base)
        m["x_t"] = np.ascontiguousarray(x[2 * core: 2 * core + 2].transpose(0, 2, 1))
        in_maps.append(m)
    return in_maps


def kernel(**inputs):
    L = 4096
    if "nc" not in _CACHE:
        _CACHE["nc"] = build(L)
    nc = _CACHE["nc"]
    in_maps = make_inputs(inputs, L, 8)
    res = run_bass_kernel_spmd(nc, in_maps, core_ids=list(range(8)))
    outs = []
    for core in range(8):
        o = res.results[core]["out_t"]
        outs.append(np.ascontiguousarray(np.asarray(o).transpose(0, 2, 1)))
    return np.concatenate(outs, axis=0).astype(np.float32)


# version 08
# speedup vs baseline: 1.1303x; 1.1303x over previous
import os
os.environ.setdefault("GAUGE_NTFF_IGNORE_DMA", "1")
import numpy as np
import ml_dtypes
import concourse.bass as bass
import concourse.mybir as mybir
from concourse.bass_utils import run_bass_kernel_spmd

F32 = mybir.dt.float32
BF16 = mybir.dt.bfloat16
AF = mybir.ActivationFunctionType
ALU = mybir.AluOpType
AX = mybir.AxisListType.X

D = 1024; NCH = 8; H = 4; DK = 256; EXP = 8; FF = 4096
TT = 256
STW = 2 * TT
CH = 128
EPS = 1e-5


def build(L, taps=()):
    NST = L // TT
    tapset = set(taps)
    nc = bass.Bass("TRN2", target_bir_lowering=False, debug=False)

    dri = lambda n, s: nc.dram_tensor(n, s, F32, kind="ExternalInput").ap()
    x_t = dri("x_t", [2, D, L])
    wq = dri("wq", [D, D]); wk = dri("wk", [D, D]); wv = dri("wv", [D, D])
    wo = dri("wo", [D, D])
    wg1 = dri("wg1", [D, D]); wg1b = dri("wg1b", [16, D])
    wog1 = dri("wog1", [D, 512]); wog1b = dri("wog1b", [8, 512])
    wb = dri("wb", [D, H]); wres = dri("wres", [D, H]); wg2 = dri("wg2", [D, 4])
    wog2 = dri("wog2", [128, 4]); gwT = dri("gwT", [D, EXP])
    cdq = dri("cdq", [NCH, 4, 128, 128]); cdk = dri("cdk", [NCH, 4, 128, 128])
    cdv = dri("cdv", [NCH, 4, 128, 128])
    cds = dri("cds", [NCH, 5, 128, 128]); cdl = dri("cdl", [NCH, 64, 128, 128])
    w1T = dri("w1T", [EXP, D, FF])
    w2r = dri("w2r", [EXP, NCH, FF, 128])
    bg1 = dri("bg1", [128, NCH]); bog1 = dri("bog1", [128, 4])
    bg2r = dri("bg2r", [128, 4]); nw = dri("nw", [128, 2])
    bres_i = dri("bres_i", [H, 1]); bog2_i = dri("bog2_i", [1, 1])
    crl_i = dri("crl_i", [H, 1]); ltemp_i = dri("ltemp_i", [1, 1])
    ident_i = dri("ident_i", [128, 128])
    masksl_i = dri("masksl_i", [128, 128]); masksu_i = dri("masksu_i", [128, 128])
    maskui_i = dri("maskui_i", [128, 128])
    sumsel_i = dri("sumsel_i", [12, 128, 68])
    rowsel4_i = dri("rowsel4_i", [4, 4, 128]); rowsel8_i = dri("rowsel8_i", [8, 8, 128])
    ones1_i = dri("ones1_i", [1, 128]); onescol_i = dri("onescol_i", [128, 1])
    statc4_i = dri("statc4_i", [4, 4 * 5 * 16])
    statc2_i = dri("statc2_i", [4, 2 * 5 * 8])
    epsc_i = dri("epsc_i", [4, 2])

    out_t = nc.dram_tensor("out_t", [2, D, L], F32, kind="ExternalOutput").ap()
    drs = lambda n, s: nc.dram_tensor(n, s, BF16, kind="Internal").ap()
    sc_attn = drs("sc_attn", [5, 128, 8192])
    sc_wog1 = drs("sc_wog1", [128, 4096])
    sc_cdqkv = drs("sc_cdqkv", [3, 128, 4096])
    sc_cdl = drs("sc_cdl", [8, 128, 8192])
    sc_cds = drs("sc_cds", [8, 128, 640])
    sc_w1 = drs("sc_w1", [EXP * 4, 128, 8192])
    sc_w2 = drs("sc_w2", [EXP * 8, 128, 4096])
    tap_drams = {}

    sb = lambda n, p, w, dt: nc.alloc_sbuf_tensor(n, [p, w], dt)
    ident = sb("ident", 128, 128, BF16)
    masksl = sb("masksl", 128, 128, BF16); masksu = sb("masksu", 128, 128, BF16)
    maskui = sb("maskui", 128, 128, BF16)
    sumsel = sb("sumsel", 128, 12 * 68, BF16)
    rowsel4 = sb("rowsel4", 4, 4 * 128, BF16); rowsel8 = sb("rowsel8", 8, 8 * 128, BF16)
    ones1f = sb("ones1f", 1, 128, F32)
    onescol = sb("onescol", 128, 1, BF16)
    statc4 = sb("statc4", 4, 4 * 5 * 16, BF16); statc2 = sb("statc2", 4, 2 * 5 * 8, BF16)
    epsc = sb("epsc", 4, 2, F32)
    wb_s = sb("wb_s", 128, NCH * 4, BF16); wres_s = sb("wres_s", 128, NCH * 4, BF16)
    wg2_s = sb("wg2_s", 128, NCH * 4, BF16); gwT_s = sb("gwT_s", 128, NCH * 8, BF16)
    wog2_s = sb("wog2_s", 128, 4, BF16)
    wg1b_s = sb("wg1b_s", 16, D, BF16); wog1b_s = sb("wog1b_s", 8, 512, BF16)
    bg1_s = sb("bg1_s", 128, NCH, F32); bog1_s = sb("bog1_s", 128, 4, F32)
    bg2r_s = sb("bg2r_s", 128, 4, F32); nw_s = sb("nw_s", 128, 2, F32)
    bres_s = sb("bres_s", H, 1, F32); bog2_s = sb("bog2_s", 1, 1, F32)
    crl_s = sb("crl_s", H, 1, F32); ltemp_s = sb("ltemp_s", 1, 1, F32)
    static_s = sb("static_s", H, 1, F32)
    tinv_s = sb("tinv_s", 1, 1, F32); tinv_bc = sb("tinv_bc", 128, 1, F32)
    tmp11 = sb("tmp11", 1, 1, F32); tmp11b = sb("tmp11b", 1, 1, F32)
    tmp11c = sb("tmp11c", 1, 1, F32); tmp11d = sb("tmp11d", 1, 1, F32)
    S_f = sb("S_f", 128, 8 * 512, F32)
    S_b = sb("S_b", 128, 8 * 512, BF16)
    rawhalo = sb("rawhalo", 128, 3 * 2 * NCH * 3, BF16)
    WSW = 69 * 128
    wsl = [sb("ws0", 128, WSW, BF16), sb("ws1", 128, WSW, BF16)]
    w2sl = [sb("w2s0", 128, 32 * 128, BF16), sb("w2s1", 128, 32 * 128, BF16)]
    x_bf = sb("x_bf", 128, NCH * STW, BF16)
    v_in = sb("v_in", 128, NCH * 2 * (63 + TT), BF16)
    shareA = sb("shareA", 128, 2 * NCH * STW, BF16)
    o_sl = sb("o_sl", 128, NCH * STW, BF16)
    rawg1 = sb("rawg1", 128, NCH * 2 * 259, BF16)
    mid_bf = sb("mid_bf", 128, NCH * STW, BF16)
    sqs = sb("sqs", 128, 4 * 512, BF16)
    spool = sb("spool", 128, NCH * 512, F32)
    xf32 = sb("xf32", 128, 2 * 512, F32)
    outstg = sb("outstg", 128, 2 * 512, F32)
    cbbc = sb("cbbc", 128, 2 * 512, BF16)
    rowsAf = sb("rowsAf", 4, 2048, F32)
    rowsAb = sb("rowsAb", 4, 4 * 512, BF16)
    rowsFb = sb("rowsFb", 4, 5 * 512, BF16)
    rows16 = sb("rows16", 16, 512, BF16)
    rows8b = sb("rows8b", 8, 512, BF16)
    rows4f = sb("rows4f", 4, 512, F32)
    pmscr = sb("pmscr", 128, 16, F32); pmscr2 = sb("pmscr2", 128, 16, F32)
    pmscr3 = sb("pmscr3", 128, 16, F32)
    pmbf = sb("pmbf", 128, 16, BF16)
    glA = sb("glA", 128, 32, F32); glB = sb("glB", 128, 32, F32)
    glC = sb("glC", 128, 32, F32); glD = sb("glD", 128, 32, F32)
    glbf = sb("glbf", 128, 32, BF16)
    gm0 = sb("gm0", 128, 4, F32); gm1 = sb("gm1", 128, 4, F32)
    gg0 = sb("gg0", 128, 4, F32); gg1 = sb("gg1", 128, 4, F32)
    dX = [[sb(f"dX{g}{p}", 128, 128, BF16) for p in range(2)] for g in range(3)]
    dU = [[sb(f"dU{g}{p}", 128, 128, BF16) for p in range(2)] for g in range(3)]
    dY = [[sb(f"dY{g}{p}", 128, 256, BF16) for p in range(2)] for g in range(3)]
    dAT = [sb(f"dAT{g}", 128, 128, BF16) for g in range(3)]
    dKB = [sb(f"dKB{g}", 128, 256, BF16) for g in range(3)]
    dVH = [sb(f"dVH{g}", 128, 256, BF16) for g in range(3)]
    dKP = [sb(f"dKP{g}", 128, 256, BF16) for g in range(3)]

    PB = [nc.alloc_psum_tensor(f"pb{i}", [128, 512], F32) for i in range(8)]

    OPS = {"tensor": [], "vector": [], "scalar": [], "gpsimd": [], "sync": []}
    SEMS = {n: [0] for n in ["pe", "dve", "act", "ldg", "ldw2", "ldw3", "wld", "ios"]}
    LABELS = {k: [] for k in OPS}
    PH = ["init"]

    def op(eng, method, *args, wait=None, inc=None, **kw):
        OPS[eng].append((method, args, kw, list(wait or []), inc))
        LABELS[eng].append(PH[0])
        if inc is not None:
            SEMS[inc[0]][0] += inc[1]

    def cur(s):
        return SEMS[s][0]

    BK = [None] * 8
    PEG = [0] * 8
    ready = {}

    def mark(key, sem):
        ready[key] = (sem, cur(sem))

    def uses(*keys):
        w = {}
        for k in keys:
            if k is None:
                continue
            s, c = k if isinstance(k, tuple) else ready.get(k, (None, 0))
            if s is None or c <= 0:
                continue
            if c > w.get(s, -1):
                w[s] = c
        return [(s, c) for s, c in w.items()]

    def epoch(bank, items, deps=()):
        w0 = uses(*deps, *(items[0][3]), BK[bank])
        n = len(items)
        for i, (o, l, r, dk) in enumerate(items):
            ww = w0 if i == 0 else (uses(*dk) if dk else None)
            op("tensor", "matmul", o, l, r, start=(i == 0), stop=(i == n - 1),
               wait=ww, inc=(("pe", 1) if i == n - 1 else None))
        PEG[bank] = cur("pe")
        BK[bank] = None

    def mm_group(bank, mms, deps=()):
        epoch(bank, [(o, l, r, []) for (o, l, r) in mms], deps=deps)

    def evac(eng, bank, method, *args, deps=(), key=None, **kw):
        sem = "dve" if eng == "vector" else "act"
        op(eng, method, *args, wait=uses(("pe", PEG[bank]), *deps), inc=(sem, 1), **kw)
        BK[bank] = (sem, cur(sem))
        if key:
            mark(key, sem)

    def veop(eng, method, *args, deps=(), key=None, **kw):
        sem = "dve" if eng == "vector" else "act"
        op(eng, method, *args, wait=uses(*deps), inc=(sem, 1), **kw)
        if key:
            mark(key, sem)

    def dma(queue, dst, src, key=None, deps=(), sem=None, **kw):
        if sem is None:
            sem = "ldg" if queue == "gpsimd" else "ios"
        op(queue, "dma_start", dst, src, wait=uses(*deps), inc=(sem, 16), **kw)
        if key:
            mark(key, sem)

    def xbf(c, lo=0, w=STW):
        return x_bf[:, c * STW + lo: c * STW + lo + w]

    def vin(c, s, lo, w):
        b = c * (2 * (63 + TT)) + s * (63 + TT) + 63 + lo
        return v_in[:, b: b + w]

    def vinr(c, s, lo, w):
        b = c * (2 * (63 + TT)) + s * (63 + TT) + lo
        return v_in[:, b: b + w]

    def vin_st(c):
        b = c * (2 * (63 + TT))
        return v_in[:, b: b + 2 * (63 + TT)].rearrange(
            "p (s t) -> p s t", s=2)[:, :, 63:63 + TT]

    def vin2(c, lo):
        b = c * (2 * (63 + TT))
        return v_in[:, b: b + 2 * (63 + TT)].rearrange(
            "p (s t) -> p s t", s=2)[:, :, lo:lo + TT]

    def shA(idx, c, lo=0, w=STW):
        b = idx * NCH * STW + c * STW + lo
        return shareA[:, b: b + w]

    def osl(c, lo=0, w=STW):
        return o_sl[:, c * STW + lo: c * STW + lo + w]

    def midv(c, lo=0, w=STW):
        return mid_bf[:, c * STW + lo: c * STW + lo + w]

    def hview(fc, lo=0, w=STW):
        if fc < 16:
            return shareA[:, fc * STW + lo: fc * STW + lo + w]
        if fc < 24:
            return o_sl[:, (fc - 16) * STW + lo: (fc - 16) * STW + lo + w]
        return x_bf[:, (fc - 24) * STW + lo: (fc - 24) * STW + lo + w]

    def raww(c, s, lo, w):
        b = c * 518 + s * 259 + lo
        return rawg1[:, b: b + w]

    def raw2(c, lo):
        b = c * 518
        return rawg1[:, b: b + 518].rearrange(
            "p (s t) -> p s t", s=2)[:, :, lo:lo + TT]

    def g1v(m, lo=0, w=512):
        return rawg1[:, m * 512 + lo: m * 512 + lo + w]

    def sel(j):
        return sumsel[:, j * 68:(j + 1) * 68]

    def rs4(h):
        return rowsel4[:, h * 128:(h + 1) * 128]

    def rs8(e):
        return rowsel8[:, e * 128:(e + 1) * 128]

    def rAb(i, lo=0, w=512):
        return rowsAb[:, i * 512 + lo: i * 512 + lo + w]

    def init_phase():
        loads = [
            (ident[:, :], ident_i), (masksl[:, :], masksl_i),
            (masksu[:, :], masksu_i), (maskui[:, :], maskui_i),
            (ones1f[:, :], ones1_i), (onescol[:, :], onescol_i),
            (statc4[:, :], statc4_i), (statc2[:, :], statc2_i), (epsc[:, :], epsc_i),
            (wb_s.rearrange("p (c h) -> p c h", h=4), wb.rearrange("(c p) h -> p c h", p=128)),
            (wres_s.rearrange("p (c h) -> p c h", h=4), wres.rearrange("(c p) h -> p c h", p=128)),
            (wg2_s.rearrange("p (c h) -> p c h", h=4), wg2.rearrange("(c p) h -> p c h", p=128)),
            (gwT_s.rearrange("p (c h) -> p c h", h=8), gwT.rearrange("(c p) h -> p c h", p=128)),
            (wog2_s[:, :], wog2),
            (wg1b_s[:, :], wg1b), (wog1b_s[:, :], wog1b),
            (bg1_s[:, :], bg1), (bog1_s[:, :], bog1), (bg2r_s[:, :], bg2r),
            (nw_s[:, :], nw), (bres_s[:, :], bres_i), (bog2_s[:, :], bog2_i),
            (crl_s[:, :], crl_i), (ltemp_s[:, :], ltemp_i),
        ]
        for j in range(12):
            loads.append((sel(j), sumsel_i[j]))
        for h in range(4):
            loads.append((rs4(h), rowsel4_i[h]))
        for e in range(8):
            loads.append((rs8(e), rowsel8_i[e]))
        for d, s in loads:
            dma("gpsimd", d, s)
        mark("init", "ldg")
        veop("vector", "memset", S_f[:, :], 0.0, key="S_f")
        veop("vector", "memset", S_b[:, :], 0.0, key="S_b")
        veop("vector", "memset", rawhalo[:, :], 0.0, key="halo")
        veop("vector", "memset", v_in[:, :], 0.0, key="v")
        veop("scalar", "activation", static_s[:, :], crl_s[:, :], AF.Sigmoid,
             deps=["init"], key="static")
        veop("scalar", "activation", tmp11[:, :], ltemp_s[:, :], AF.Abs, deps=["init"])
        veop("scalar", "activation", tmp11b[:, :], tmp11[:, :], AF.Exp, scale=-1.0)
        veop("scalar", "activation", tmp11c[:, :], tmp11b[:, :], AF.Ln, bias=1.0)
        veop("scalar", "activation", tmp11d[:, :], ltemp_s[:, :], AF.Relu, deps=["init"])
        veop("vector", "tensor_add", tmp11[:, :], tmp11c[:, :], tmp11d[:, :],
             deps=[("act", cur("act"))])
        veop("vector", "reciprocal", tinv_s[:, :], tmp11[:, :],
             deps=[("dve", cur("dve"))], key="tinv")
        mm_group(7, [(PB[7][0:128, 0:1], ones1f[:, :], tinv_s[:, :])],
                 deps=["tinv", "init"])
        evac("vector", 7, "tensor_copy", tinv_bc[:, :], PB[7][0:128, 0:1], key="tinvbc")
        itemsA = [
            ("scwq", sc_attn[0], wq.rearrange("(kc p) m -> p kc m", p=128)),
            ("sccdq", sc_cdqkv[0], cdq.rearrange("c t p q -> p c t q")),
            ("scwk", sc_attn[1], wk.rearrange("(kc p) m -> p kc m", p=128)),
            ("sccdk", sc_cdqkv[1], cdk.rearrange("c t p q -> p c t q")),
            ("scwv", sc_attn[2], wv.rearrange("(kc p) m -> p kc m", p=128)),
            ("sccdv", sc_cdqkv[2], cdv.rearrange("c t p q -> p c t q")),
        ]
        for c in range(NCH):
            itemsA.append((f"sccdl{c}", sc_cdl[c], cdl[c].rearrange("t p q -> p t q")))
            itemsA.append((f"sccds{c}", sc_cds[c], cds[c].rearrange("t p q -> p t q")))
        itemsA += [
            ("scwg1", sc_attn[3], wg1.rearrange("(kc p) m -> p kc m", p=128)),
            ("scwog1", sc_wog1, wog1.rearrange("(kc p) m -> p kc m", p=128)),
            ("scwo", sc_attn[4], wo.rearrange("(kc p) m -> p kc m", p=128)),
        ]
        for i, (nm, dst, srcap) in enumerate(itemsA):
            si = i % 2
            shp = list(srcap.shape)
            ncols = 1
            for dd in shp[1:]:
                ncols *= dd
            slotv = wsl[si][:, 0:ncols]
            if len(shp) == 3:
                slotv = slotv.rearrange("p (a b) -> p a b", a=shp[1])
            elif len(shp) == 4:
                slotv = slotv.rearrange("p (a b c) -> p a b c", a=shp[1], b=shp[2])
            dma("gpsimd", slotv, srcap, deps=[("ldg", cur("ldg"))])
            dma("gpsimd", dst, wsl[si][:, 0:ncols], deps=[("ldg", cur("ldg"))])
            mark(nm, "ldg")
        mark("convA", "ldg")
        for e in range(EXP):
            for fs in range(4):
                for hh in range(2):
                    srcap = w1T[e][hh * 512:(hh + 1) * 512,
                                   fs * 1024:(fs + 1) * 1024].rearrange(
                                       "(kc p) f -> p kc f", p=128)
                    dma("gpsimd", w2sl[0][:, 0:4096].rearrange(
                            "p (a b) -> p a b", a=4),
                        srcap, deps=[("ldw2", cur("ldw2"))], sem="ldw2")
                    dma("gpsimd", sc_w1[e * 4 + fs][:, hh * 4096:(hh + 1) * 4096],
                        w2sl[0][:, 0:4096], deps=[("ldw2", cur("ldw2"))], sem="ldw2")
                mark(f"scw1_{e}_{fs}", "ldw2")
        for e in range(EXP):
            for do in range(NCH):
                dma("gpsimd", w2sl[1][:, 0:4096].rearrange(
                        "p (a b) -> p a b", a=32),
                    w2r[e, do].rearrange("(kc p) m -> p kc m", p=128),
                    deps=[("ldw3", cur("ldw3"))], sem="ldw3")
                dma("gpsimd", sc_w2[e * 8 + do], w2sl[1][:, 0:4096],
                    deps=[("ldw3", cur("ldw3"))], sem="ldw3")
                mark(f"scw2_{e}_{do}", "ldw3")
        mark("convB", "ldw2")
        mark("convB2", "ldw3")

    slot_last = {0: None, 1: None, 10: None, 11: None}

    def tap(name, st, ap):
        if name not in tapset or st != 0:
            return
        shp = list(ap.shape)
        d = nc.dram_tensor(f"tap_{name}", shp, F32, kind="ExternalOutput").ap()
        tap_drams[name] = d
        dma("gpsimd", d, ap,
            deps=[("pe", cur("pe")), ("dve", cur("dve")), ("act", cur("act")),
                  ("ios", cur("ios"))])

    def phase_proj(st):
        t0 = st * TT
        for c in range(NCH):
            for s in range(2):
                veop("vector", "tensor_copy", vinr(c, s, 0, 63), vin(c, s, TT - 63, 63),
                     deps=(["v", "v_dead_pe"] if st > 0 else ["v"]))
        mark("vrot", "dve")
        for c in range(NCH):
            dma("gpsimd", xbf(c).rearrange("p (s t) -> p s t", s=2),
                x_t[:, c * 128:(c + 1) * 128, t0:t0 + TT].rearrange("s p t -> p s t"),
                sem="wld", deps=(["xbf_used"] if c == 0 else []))
        mark("x", "wld")

        projs = [("q", 0, 0), ("k", 1, 1), ("v", 2, 2)]
        for (pn, wi, pidx) in projs:
            dma("sync", wsl[0][:, 0:8192], sc_attn[wi], key="wP", sem="wld",
                deps=["convA", "scw" + pn] + ([slot_last[0]] if slot_last[0] else []))
            dma("sync", wsl[1][:, 0:4 * NCH * 128], sc_cdqkv[pidx], key="cdP",
                sem="wld",
                deps=["convA", "sccd" + pn] + ([slot_last[1]] if slot_last[1] else []))
            for m in range(NCH):
                b = m % 2
                mm_group(b, [(PB[b][:, 0:512],
                              wsl[0][:, k * 1024 + m * 128: k * 1024 + m * 128 + 128],
                              xbf(k)) for k in range(NCH)],
                         deps=["wP", "x"])
                for s in range(2):
                    evac("scalar", b, "tensor_copy", raww(m, s, 3, 256),
                         PB[b][:, s * 256: s * 256 + 256], key=f"raw{m}")
            slot_last[0] = ("pe", cur("pe"))
            for c in range(NCH):
                for s in range(2):
                    hb = pidx * (2 * NCH * 3) + s * (NCH * 3) + c * 3
                    veop("vector", "tensor_copy", raww(c, s, 0, 3),
                         rawhalo[:, hb:hb + 3],
                         deps=["halo", f"raw{c}", ("act", cur("act"))])
                    veop("vector", "tensor_copy", rawhalo[:, hb:hb + 3],
                         raww(c, s, 256, 3), deps=[f"raw{c}"])
            mark("halo", "dve")
            mark("rawfill", "dve")
            for c in range(NCH):
                b = 2 + (c % 2)
                mm_group(b, [(PB[b][:, 0:512],
                              wsl[1][:, (c * 4 + t) * 128:(c * 4 + t) * 128 + 128],
                              raw2(c, t)) for t in range(4)],
                         deps=["cdP", f"raw{c}", "rawfill"])
                if pn == "v":
                    for s in range(2):
                        evac("scalar", b, "activation", vin(c, s, 0, 256),
                             PB[b][:, s * 256:s * 256 + 256], AF.Silu,
                             deps=["vrot", "v_dead_dve"], key="v")
                else:
                    idx = 0 if pn == "q" else 1
                    evac("scalar", b, "activation", shA(idx, c),
                         PB[b][:, 0:512], AF.Silu,
                         deps=["ls_dead", "ll_dead"], key=("qs" if pn == "q" else "ks"))
            slot_last[1] = ("pe", cur("pe"))
        for idx, nm, okey in ((0, "qs", "qn"), (1, "ks", "kn")):
            items = []
            pe_ck = {}
            for c in range(NCH):
                ping = c % 2
                sq_t = sqs[:, ping * 512:(ping + 1) * 512]
                veop("scalar", "activation", sq_t, shA(idx, c), AF.Square,
                     deps=[nm, ("pe", pe_ck.get(c - 2, cur("pe") if c < 2 else 0))],
                     key=f"sq{c}")
                items.append((PB[4][0:68, :], sel(c // 2), sq_t, [f"sq{c}"]))
            w0 = uses(*(items[0][3]), BK[4])
            for i, (o, l, r, dk) in enumerate(items):
                op("tensor", "matmul", o, l, r, start=(i == 0),
                   stop=(i == len(items) - 1),
                   wait=(w0 if i == 0 else uses(*dk)),
                   inc=("pe", 1))
                pe_ck[i] = cur("pe")
            PEG[4] = cur("pe"); BK[4] = None
            evac("scalar", 4, "activation", rows4f[:, 0:512], PB[4][0:4, :], AF.Sqrt,
                 bias=epsc[0:4, 0:1], deps=["init"], key="l2a")
            veop("vector", "reciprocal", rAb(3), rows4f[:, 0:512],
                 deps=["l2a"], key="l2r")
            for h in range(4):
                mm_group(5, [(PB[5][:, 0:512], rs4(h), rAb(3))],
                         deps=["l2r"])
                for cc in (2 * h, 2 * h + 1):
                    evac("vector", 5, "tensor_mul", shA(idx, cc), shA(idx, cc),
                         PB[5][:, 0:512], deps=[nm], key=okey)
        mms = [(PB[6][0:4, :], wb_s[:, k * 4:(k + 1) * 4], xbf(k)) for k in range(NCH)]
        mms += [(PB[6][32:36, :], wres_s[:, k * 4:(k + 1) * 4], xbf(k)) for k in range(NCH)]
        mm_group(6, mms, deps=["x", "init"])
        evac("scalar", 6, "activation", rAb(0), PB[6][0:4, :], AF.Sigmoid, key="beta")
        tap("xbf", st, x_bf[:, :])
        tap("qn", st, shareA[:, 0:NCH * STW])
        tap("kn", st, shareA[:, NCH * STW:2 * NCH * STW])
        tap("vin", st, v_in[:, :])
        evac("scalar", 6, "activation", rAb(2), PB[6][32:36, :], AF.Sigmoid,
             bias=bres_s[:, 0:1], key="rg")
        veop("vector", "tensor_scalar", rAb(2), rAb(2), static_s[:, 0:1], None,
             ALU.mult, deps=["rg", "static"], key="rg")
        tap("beta", st, rowsAb[0:4, 0:512])
        tap("rg", st, rowsAb[0:4, 1024:1536])

    def delta_chunk(st, s, h, i):
        g = (s * 4 + h) % 3
        bP, bQ = 2 * g, 2 * g + 1
        oS = (s * 4 + h) * 512
        pc = s * 256 + i * 128

        def knc(k):
            return shA(1, 2 * h + k, pc, 128)

        def qnc(k):
            return shA(0, 2 * h + k, pc, 128)

        def Sb(k, lo=0, w=256):
            return S_b[:, oS + k * 256 + lo: oS + k * 256 + lo + w]

        mm_group(bP, [(PB[bP][:, 0:128], rs4(h), rAb(0, pc, 128))], deps=["beta"])
        for k in range(2):
            evac("vector", bP, "scalar_tensor_tensor", dKB[g][:, k * 128:(k + 1) * 128],
                 knc(k), -1.0, PB[bP][:, 0:128], ALU.mult, ALU.mult,
                 deps=["kn"], key=f"kb{g}")
            evac("vector", bP, "tensor_mul", dVH[g][:, k * 128:(k + 1) * 128],
                 vin(2 * h + k, s, i * 128, 128), PB[bP][:, 0:128],
                 deps=["v"], key=f"vh{g}")
        mm_group(bP, [(PB[bP][:, 0:128], dKB[g][:, k * 128:(k + 1) * 128], knc(k))
                      for k in range(2)] +
                     [(PB[bP][:, 128:256], knc(k), dKB[g][:, k * 128:(k + 1) * 128])
                      for k in range(2)],
                 deps=[f"kb{g}", "kn"])
        evac("vector", bP, "tensor_mul", dX[g][0][:, :], PB[bP][:, 0:128],
             masksl[:, :], key=f"X{g}")
        op("vector", "tensor_mul", dU[g][0][:, :], PB[bP][:, 128:256], masksu[:, :],
           wait=uses(("pe", PEG[bP])), inc=("dve", 1))
        mark(f"U{g}", "dve")
        BK[bP] = ("dve", cur("dve"))
        mm_group(bQ, [(PB[bQ][:, k * 128:(k + 1) * 128],
                       dVH[g][:, k * 128:(k + 1) * 128], ident[:, :])
                      for k in range(2)] +
                     [(PB[bQ][:, 0:256], dKB[g][:, k * 128:(k + 1) * 128], Sb(k))
                      for k in range(2)],
                 deps=[f"kb{g}", f"vh{g}", "S_b"])
        evac("scalar", bQ, "tensor_copy", dY[g][0][:, :], PB[bQ][:, 0:256], key=f"y{g}")
        for j in range(7):
            pj = j % 2
            mm_group(bP, [(PB[bP][:, 0:256], dU[g][pj][:, :], dY[g][pj][:, :]),
                          (PB[bP][:, 0:256], ident[:, :], dY[g][pj][:, :])],
                     deps=[f"U{g}", f"y{g}"])
            evac("scalar", bP, "tensor_copy", dY[g][1 - pj][:, :], PB[bP][:, 0:256],
                 key=f"y{g}")
            if j < 6:
                mm_group(bQ, [(PB[bQ][:, 0:128], dU[g][pj][:, :], dX[g][pj][:, :]),
                              (PB[bQ][:, 128:256], dX[g][pj][:, :], dU[g][pj][:, :])],
                         deps=[f"X{g}", f"U{g}"])
                evac("vector", bQ, "tensor_copy", dX[g][1 - pj][:, :], PB[bQ][:, 0:128],
                     key=f"X{g}")
                op("scalar", "tensor_copy", dU[g][1 - pj][:, :], PB[bQ][:, 128:256],
                   wait=uses(("pe", PEG[bQ])), inc=("act", 1))
                mark(f"U{g}", "act")
                BK[bQ] = ("act", cur("act"))
        ui2 = dY[g][1]
        mm_group(bP, [(PB[bP][:, 0:128], knc(k), qnc(k)) for k in range(2)],
                 deps=["qn", "kn"])
        evac("vector", bP, "tensor_mul", dAT[g][:, :], PB[bP][:, 0:128],
             maskui[:, :], key=f"at{g}")
        mms = []
        for m in range(2):
            mms.append((PB[bQ][:, m * 128:(m + 1) * 128], Sb(0, m * 128, 128), qnc(0)))
            mms.append((PB[bQ][:, m * 128:(m + 1) * 128], Sb(1, m * 128, 128), qnc(1)))
            mms.append((PB[bQ][:, m * 128:(m + 1) * 128],
                        ui2[:, m * 128:(m + 1) * 128], dAT[g][:, :]))
        mm_group(bQ, mms, deps=[f"at{g}", f"y{g}", "S_b", "qn"])
        for m in range(2):
            evac("scalar", bQ, "tensor_copy", osl(2 * h + m, pc, 128),
                 PB[bQ][:, m * 128:(m + 1) * 128], key="delta")
        mm_group(bP, [(PB[bP][:, k * 128:(k + 1) * 128], knc(k), ident[:, :])
                      for k in range(2)], deps=["kn"])
        evac("scalar", bP, "tensor_copy", dKP[g][:, :], PB[bP][:, 0:256], key=f"kp{g}")
        mm_group(bQ, [(PB[bQ][:, k * 256:(k + 1) * 256],
                       dKP[g][:, k * 128:(k + 1) * 128], ui2[:, :]) for k in range(2)],
                 deps=[f"kp{g}", f"y{g}"])
        for k in range(2):
            evac("vector", bQ, "tensor_add", S_f[:, oS + k * 256: oS + k * 256 + 256],
                 S_f[:, oS + k * 256: oS + k * 256 + 256],
                 PB[bQ][:, k * 256:(k + 1) * 256], deps=["S_f", "S_b"], key="S_f")
        for k in range(2):
            veop("scalar", "tensor_copy", Sb(k), S_f[:, oS + k * 256: oS + k * 256 + 256],
                 deps=["S_f"], key="S_b")

    def phase_delta(st):
        for i in range(TT // CH):
            for s in range(2):
                for h in range(4):
                    delta_chunk(st, s, h, i)
        mark("qn_dead", "pe")
        mark("kn_dead", "pe")
        tap("delta", st, o_sl[:, :])
        tap("S", st, S_f[:, :])

    def phase_fir(st):
        for c in range(NCH):
            si = c % 2
            dma("sync", wsl[si][:, 0:64 * 128], sc_cdl[c],
                key=f"fir{c}", sem="wld",
                deps=["convA", f"sccdl{c}"] + ([slot_last[si]] if slot_last[si] else []))
            dma("sync", wsl[si][:, 64 * 128:69 * 128], sc_cds[c], key=f"firs{c}",
                sem="wld",
                deps=["convA", f"sccds{c}"] + ([slot_last[si]] if slot_last[si] else []))
            b = c % 2
            mm_group(b, [(PB[b][:, 0:512], wsl[si][:, t * 128:(t + 1) * 128],
                          vin2(c, t)) for t in range(64)],
                     deps=[f"fir{c}", "v", "vrot"])
            evac("scalar", b, "tensor_copy", shA(1, c), PB[b][:, 0:512],
                 deps=["kn_dead"], key="ll")
            b2 = 2 + (c % 2)
            mm_group(b2, [(PB[b2][:, 0:512],
                           wsl[si][:, (64 + t) * 128:(64 + t + 1) * 128],
                           vin2(c, 59 + t)) for t in range(5)],
                      deps=[f"firs{c}", "v", "vrot"])
            evac("scalar", b2, "tensor_copy", shA(0, c), PB[b2][:, 0:512],
                 deps=["qn_dead"], key="ls")
            slot_last[si] = ("pe", cur("pe"))
            slot_last[10 + (c % 2)] = ("pe", cur("pe"))
        tap("ls", st, shareA[:, 0:NCH * STW])
        tap("ll", st, shareA[:, NCH * STW:2 * NCH * STW])

    def stats_emit(branches, coef, width, outbank, outrows, outkey, outbuf):
        first_epoch = True
        nb = len(branches)
        for b, (bview, bkeys) in enumerate(branches):
            items = []
            pe_ck = {}
            for c in range(NCH):
                ping = c % 2
                sq_t = sqs[:, ping * 1024: ping * 1024 + 512]
                ab_t = sqs[:, ping * 1024 + 512: ping * 1024 + 1024]
                scrdep = ("pe", pe_ck.get(3 * (c - 2) + 2, cur("pe") if c < 2 else 0))
                veop("scalar", "activation", sq_t, bview(c), AF.Square,
                     deps=list(bkeys) + [scrdep], key=f"stsq{c}")
                veop("scalar", "activation", ab_t, bview(c), AF.Abs,
                     deps=list(bkeys), key=f"stab{c}")
                items.append((PB[6][0:68, :], sel(c // 2), bview(c), list(bkeys)))
                items.append((PB[6][0:68, :], sel(4 + c // 2), sq_t, [f"stsq{c}"]))
                items.append((PB[6][0:68, :], sel(8 + c // 2), ab_t, [f"stab{c}"]))
            w0 = uses(*(items[0][3]), BK[6])
            for i, (o, l, r, dk) in enumerate(items):
                op("tensor", "matmul", o, l, r, start=(i == 0),
                   stop=(i == len(items) - 1),
                   wait=(w0 if i == 0 else uses(*dk)), inc=("pe", 1))
                pe_ck[i] = cur("pe")
            PEG[6] = cur("pe"); BK[6] = None
            evac("scalar", 6, "tensor_copy", rowsFb[:, 0:512], PB[6][0:4, :], key="fA")
            op("scalar", "activation", rowsFb[:, 512:1024], PB[6][0:4, :], AF.Square,
               wait=uses(("pe", PEG[6])), inc=("act", 1)); mark("fsA", "act")
            op("scalar", "tensor_copy", rowsFb[:, 1024:1536], PB[6][32:36, :],
               wait=uses(("pe", PEG[6])), inc=("act", 1)); mark("fB", "act")
            op("scalar", "activation", rowsFb[:, 1536:2048], PB[6][32:36, :], AF.Sqrt,
               wait=uses(("pe", PEG[6])), inc=("act", 1)); mark("fsB", "act")
            op("scalar", "tensor_copy", rowsFb[:, 2048:2560], PB[6][64:68, :],
               wait=uses(("pe", PEG[6])), inc=("act", 1)); mark("fC", "act")
            BK[6] = ("act", cur("act"))
            cof = lambda w: coef[:, (b * 5 + w) * width:(b * 5 + w + 1) * width]
            items2 = [(cof(0), rowsFb[:, 0:512], ["fA", "init"]),
                      (cof(1), rowsFb[:, 512:1024], ["fsA"]),
                      (cof(2), rowsFb[:, 1024:1536], ["fB"]),
                      (cof(3), rowsFb[:, 1536:2048], ["fsB"]),
                      (cof(4), rowsFb[:, 2048:2560], ["fC"])]
            for i, (l, r, dk) in enumerate(items2):
                ww = uses(*dk, BK[outbank]) if (first_epoch and i == 0) else uses(*dk)
                st_ = first_epoch and i == 0
                sp_ = (b == nb - 1) and (i == len(items2) - 1)
                op("tensor", "matmul", PB[outbank][0:outrows, :], l, r,
                   start=st_, stop=sp_, wait=ww, inc=("pe", 1))
            first_epoch = False
        PEG[outbank] = cur("pe"); BK[outbank] = None
        evac("scalar", outbank, "tensor_copy", outbuf, PB[outbank][0:outrows, :],
             key=outkey)

    def phase_stats(st):
        branches = [(lambda c: shA(0, c), ["ls"]),
                    (lambda c: shA(1, c), ["ll"]),
                    (lambda c: osl(c), ["delta"]),
                    (lambda c: vin_st(c), ["v"])]
        stats_emit(branches, statc4, 16, 7, 16, "st16", rows16[:, :])
        tap("st16", st, rows16[:, :])

    def phase_gates(st):
        dma("sync", wsl[0][:, 0:8192], sc_attn[3], key="wg1l", sem="wld",
            deps=["convA"] + ([slot_last[0]] if slot_last[0] else []))
        for m in range(NCH):
            b = m % 2
            mms = [(PB[b][:, 0:512],
                    wsl[0][:, k * 1024 + m * 128: k * 1024 + m * 128 + 128], xbf(k))
                   for k in range(NCH)]
            mms.append((PB[b][:, 0:512], wg1b_s[:, m * 128:(m + 1) * 128], rows16[:, :]))
            mm_group(b, mms, deps=["wg1l", "x", "st16"])
            evac("scalar", b, "activation", g1v(m), PB[b][:, 0:512], AF.Gelu,
                 bias=bg1_s[:, m:m + 1], key="g1")
        slot_last[0] = ("pe", cur("pe"))
        mms = []
        for tc in range(4):
            for k in range(NCH):
                mms.append((PB[2][:, tc * 4:(tc + 1) * 4],
                            g1v(k, tc * 128, 128), wg2_s[:, k * 4:(k + 1) * 4]))
        mm_group(2, mms, deps=["g1", "init"])
        mark("g1_dead", "pe")
        for tc in range(4):
            t4 = slice(tc * 4, (tc + 1) * 4)
            evac("vector", 2, "tensor_add", pmscr[:, t4], PB[2][:, t4], bg2r_s[:, :],
                 deps=["init"])
            veop("vector", "tensor_scalar", pmscr[:, t4], pmscr[:, t4],
                 tinv_bc[:, 0:1], None, ALU.mult, deps=["tinvbc"])
            veop("vector", "tensor_reduce", gm0[:, tc:tc + 1], pmscr[:, t4], AX,
                 ALU.max, deps=[("dve", cur("dve"))])
            veop("vector", "tensor_scalar", pmscr2[:, t4], pmscr[:, t4],
                 gm0[:, tc:tc + 1], None, ALU.subtract, deps=[("dve", cur("dve"))])
            veop("scalar", "activation", pmscr3[:, t4], pmscr2[:, t4], AF.Exp,
                 deps=[("dve", cur("dve"))])
            veop("vector", "tensor_reduce", gm1[:, tc:tc + 1], pmscr3[:, t4], AX,
                 ALU.add, deps=[("act", cur("act"))])
            veop("vector", "reciprocal", gg0[:, tc:tc + 1], gm1[:, tc:tc + 1],
                 deps=[("dve", cur("dve"))])
            veop("vector", "tensor_scalar", pmscr3[:, t4], pmscr3[:, t4],
                 gg0[:, tc:tc + 1], None, ALU.mult, deps=[("dve", cur("dve"))])
            veop("vector", "tensor_scalar_max", pmscr3[:, t4], pmscr3[:, t4], 0.02)
            veop("vector", "tensor_reduce", gg1[:, tc:tc + 1], pmscr3[:, t4], AX,
                 ALU.add, deps=[("dve", cur("dve"))])
            veop("vector", "reciprocal", gm1[:, tc:tc + 1], gg1[:, tc:tc + 1],
                 deps=[("dve", cur("dve"))])
            veop("vector", "tensor_scalar", pmscr3[:, t4], pmscr3[:, t4],
                 gm1[:, tc:tc + 1], None, ALU.mult, deps=[("dve", cur("dve"))])
            veop("scalar", "tensor_copy", pmbf[:, t4], pmscr3[:, t4],
                 deps=[("dve", cur("dve"))], key="pmb")
        tap("pmscr", st, pmscr[:, :])
        tap("pmscr2", st, pmscr3[:, :])
        tap("gm1", st, gm1[:, :])
        tap("tinvbc", st, tinv_bc[:, :])
        mm_group(3, [(PB[3][0:4, tc * 128:(tc + 1) * 128], pmbf[:, tc * 4:(tc + 1) * 4],
                      ident[:, :]) for tc in range(4)], deps=["pmb", "init"])
        evac("scalar", 3, "tensor_copy", rAb(1), PB[3][0:4, 0:512], key="prow")
        for j, bk in ((1, 4), (2, 5), (3, 6)):
            mm_group(bk, [(PB[bk][:, 0:512], rs4(j), rAb(1))], deps=["prow"])
        t1 = sqs[:, 0:512]; t2 = sqs[:, 512:1024]; t3 = sqs[:, 1024:1536]
        for h in range(4):
            mm_group(3, [(PB[3][:, 0:512], rs4(0), rAb(1)),
                         (PB[3][:, 0:512], rs4(h), rAb(2))], deps=["prow", "rg"])
            for cc in (2 * h, 2 * h + 1):
                veop("vector", "tensor_mul", t1, shA(0, cc), PB[3][:, 0:512],
                     deps=["ls", ("pe", PEG[3]), ("pe", cur("pe"))])
                veop("vector", "tensor_mul", t2, shA(1, cc), PB[4][:, 0:512],
                     deps=["ll", ("pe", PEG[4])])
                veop("vector", "tensor_add", t1, t1, t2)
                veop("vector", "tensor_mul", t2, osl(cc), PB[5][:, 0:512],
                     deps=["delta", ("pe", PEG[5])])
                veop("vector", "tensor_mul", t3, vin_st(cc), PB[6][:, 0:512],
                     deps=["v", ("pe", PEG[6])])
                veop("vector", "tensor_add", t2, t2, t3)
                veop("vector", "tensor_add", osl(cc), t1, t2, key="omix")
            BK[3] = ("dve", cur("dve"))
        BK[4] = BK[5] = BK[6] = ("dve", cur("dve"))
        mark("ls_dead", "dve")
        mark("ll_dead", "dve")
        tap("g1", st, rawg1[:, 0:4096])
        tap("prow", st, rowsAb[0:4, 512:1024])
        tap("omix", st, o_sl[:, :])

    def phase_og(st):
        branches = [(lambda c: osl(c), ["omix"]), (lambda c: vin_st(c), ["v"])]
        stats_emit(branches, statc2, 8, 7, 8, "og8", rows8b[:, :])
        mark("v_dead_pe", "pe")
        mark("v_dead_dve", "dve")
        dma("sync", wsl[1][:, 0:4096], sc_wog1, key="wog1l", sem="wld",
            deps=["convA"] + ([slot_last[1]] if slot_last[1] else []))
        for m in range(4):
            b = m % 2
            mms = [(PB[b][:, 0:512],
                    wsl[1][:, k * 512 + m * 128: k * 512 + m * 128 + 128], xbf(k))
                   for k in range(NCH)]
            mms.append((PB[b][:, 0:512], wog1b_s[:, m * 128:(m + 1) * 128], rows8b[:, :]))
            mm_group(b, mms, deps=["wog1l", "x", "og8"])
            evac("scalar", b, "activation", g1v(m), PB[b][:, 0:512], AF.Gelu,
                 bias=bog1_s[:, m:m + 1], key="og1")
        slot_last[1] = ("pe", cur("pe"))
        mm_group(2, [(PB[2][0:1, 0:512], wog2_s[:, k:k + 1], g1v(k))
                     for k in range(4)], deps=["og1", "init"])
        mark("xbf_dead", "pe")
        evac("scalar", 2, "activation", rowsAf[0:1, 0:512], PB[2][0:1, 0:512],
             AF.Sigmoid, bias=bog2_s[0:1, 0:1], key="ogr")
        mm_group(3, [(PB[3][:, 0:512], ones1f[:, :], rowsAf[0:1, 0:512])], deps=["ogr"])
        for c in range(NCH):
            evac("vector", 3, "tensor_mul", osl(c), osl(c), PB[3][:, 0:512],
                 deps=["omix"], key="og_o")
        items = []
        pe_ck = {}
        for c in range(NCH):
            ping = c % 2
            sq_t = sqs[:, ping * 512:(ping + 1) * 512]
            veop("scalar", "activation", sq_t, osl(c), AF.Square,
                 deps=["og_o", ("pe", pe_ck.get(c - 2, cur("pe") if c < 2 else 0))],
                 key=f"rsq{c}")
            items.append((PB[4][0:68, :], sel(c // 2), sq_t, [f"rsq{c}"]))
        w0 = uses(*(items[0][3]), BK[4])
        for i, (o, l, r, dk) in enumerate(items):
            op("tensor", "matmul", o, l, r, start=(i == 0), stop=(i == len(items) - 1),
               wait=(w0 if i == 0 else uses(*dk)), inc=("pe", 1))
            pe_ck[i] = cur("pe")
        PEG[4] = cur("pe"); BK[4] = None
        evac("scalar", 4, "activation", rows4f[:, 0:512], PB[4][0:4, :], AF.Sqrt,
             scale=1.0 / 256.0, bias=epsc[0:4, 1:2], deps=["init"], key="rmsa")
        veop("vector", "reciprocal", rAb(3), rows4f[:, 0:512],
             deps=["rmsa"], key="rmsr")
        for h in range(4):
            mm_group(5, [(PB[5][:, 0:512], rs4(h), rAb(3))], deps=["rmsr"])
            for cc in (2 * h, 2 * h + 1):
                evac("vector", 5, "scalar_tensor_tensor", osl(cc), osl(cc),
                     nw_s[:, (cc % 2):(cc % 2) + 1], PB[5][:, 0:512],
                     ALU.mult, ALU.mult, deps=["og_o", "init"], key="on")
        tap("og8", st, rows8b[:, :])
        tap("on", st, o_sl[:, :])

    def ln_rows(srckey, srcbuf):
        items = []
        pe_ck = {}
        for c in range(NCH):
            ping = c % 2
            sb_t = sqs[:, ping * 1024: ping * 1024 + 512]
            sq_t = sqs[:, ping * 1024 + 512: ping * 1024 + 1024]
            scrdep = ("pe", pe_ck.get(2 * (c - 2) + 1, cur("pe") if c < 2 else 0))
            veop("scalar", "tensor_copy", sb_t, srcbuf[:, c * 512:(c + 1) * 512],
                 deps=[srckey, scrdep], key=f"lnc{c}")
            veop("scalar", "activation", sq_t, sb_t, AF.Square, key=f"lns{c}")
            items.append((PB[2][0:1, :], onescol[:, :], sb_t, [f"lnc{c}", "init"]))
            items.append((PB[2][32:33, :], onescol[:, :], sq_t, [f"lns{c}"]))
        w0 = uses(*(items[0][3]), BK[2])
        for i, (o, l, r, dk) in enumerate(items):
            op("tensor", "matmul", o, l, r, start=(i == 0), stop=(i == len(items) - 1),
               wait=(w0 if i == 0 else uses(*dk)), inc=("pe", 1))
            pe_ck[i] = cur("pe")
        PEG[2] = cur("pe"); BK[2] = None
        mean_r = rowsAf[0:1, 0:512]; msq_r = rowsAf[0:1, 512:1024]
        var_r = rowsAf[0:1, 1024:1536]; rstd_r = rowsAf[0:1, 1536:2048]
        m2_r = rows4f[0:1, 0:512]; sd_r = rowsAf[0:1, 512:1024]
        evac("vector", 2, "tensor_scalar", mean_r, PB[2][0:1, :], 1.0 / D, None,
             ALU.mult, key="lnm")
        op("vector", "tensor_scalar", msq_r, PB[2][32:33, :], 1.0 / D, None, ALU.mult,
           wait=uses(("pe", PEG[2])), inc=("dve", 1))
        BK[2] = ("dve", cur("dve"))
        veop("vector", "tensor_mul", m2_r, mean_r, mean_r,
             deps=[("dve", cur("dve"))])
        veop("vector", "tensor_sub", var_r, msq_r, m2_r,
             deps=[("dve", cur("dve"))])
        veop("scalar", "activation", sd_r, var_r, AF.Sqrt, bias=epsc[0:1, 1:2],
             deps=[("dve", cur("dve")), "init"])
        veop("vector", "reciprocal", rstd_r, sd_r,
             deps=[("act", cur("act")), ("dve", cur("dve"))], key="lnr")
        mm_group(3, [(PB[3][:, 0:512], ones1f[:, :], mean_r)], deps=["lnm", "lnr"])
        mm_group(4, [(PB[4][:, 0:512], ones1f[:, :], rstd_r)], deps=["lnr"])

    def phase_ln1(st):
        t0 = st * TT
        dma("sync", wsl[0][:, 0:8192], sc_attn[4], key="wol", sem="wld",
            deps=["convA"] + ([slot_last[0]] if slot_last[0] else []))
        for m in range(NCH):
            b = m % 2
            mm_group(b, [(PB[b][:, 0:512],
                          wsl[0][:, k * 1024 + m * 128: k * 1024 + m * 128 + 128],
                          osl(k)) for k in range(NCH)],
                     deps=["wol", "on"])
            ping = m % 2
            dma("sync", xf32[:, ping * 512:(ping + 1) * 512].rearrange(
                "p (s t) -> p s t", s=2),
                x_t[:, m * 128:(m + 1) * 128, t0:t0 + TT].rearrange("s p t -> p s t"),
                key=f"xf{m}", deps=([f"xfdead{ping}"] if m >= 2 else []))
            evac("vector", b, "tensor_add", spool[:, m * 512:(m + 1) * 512],
                 xf32[:, ping * 512:(ping + 1) * 512], PB[b][:, 0:512],
                 deps=[f"xf{m}"], key="ssum")
            mark(f"xfdead{ping}", "dve")
        slot_last[0] = ("pe", cur("pe"))
        mark("o_dead", "pe")
        tap("ssum", st, spool[:, :])
        ln_rows("ssum", spool)
        for c in range(NCH):
            veop("vector", "tensor_sub", xf32[:, 0:512],
                 spool[:, c * 512:(c + 1) * 512], PB[3][:, 0:512],
                 deps=["ssum", ("pe", PEG[3]), "xfdead0"])
            mark("xfdead0", "dve")
            veop("vector", "tensor_mul", midv(c), xf32[:, 0:512], PB[4][:, 0:512],
                 deps=[("pe", PEG[4]), "mid_dead"], key="mid")
        BK[3] = ("dve", cur("dve")); BK[4] = ("dve", cur("dve"))
        tap("mid", st, mid_bf[:, :])

    def phase_ffn(st):
        t0 = st * TT
        mms = []
        for tc in range(4):
            for k in range(NCH):
                mms.append((PB[5][:, tc * 8:(tc + 1) * 8],
                            midv(k, tc * 128, 128), gwT_s[:, k * 8:(k + 1) * 8]))
        mm_group(5, mms, deps=["mid", "init"])
        for tc in range(4):
            t8 = slice(tc * 8, (tc + 1) * 8)
            evac("vector", 5, "tensor_copy", glA[:, t8], PB[5][:, t8])
            veop("vector", "tensor_reduce", gm0[:, tc:tc + 1], glA[:, t8], AX,
                 ALU.max, deps=[("dve", cur("dve"))])
            veop("vector", "tensor_scalar", glB[:, t8], glA[:, t8],
                 gm0[:, tc:tc + 1], None, ALU.is_equal, deps=[("dve", cur("dve"))])
            veop("vector", "scalar_tensor_tensor", glC[:, t8], glB[:, t8], -1e30,
                 glA[:, t8], ALU.mult, ALU.add, deps=[("dve", cur("dve"))])
            veop("vector", "tensor_reduce", gm1[:, tc:tc + 1], glC[:, t8], AX,
                 ALU.max, deps=[("dve", cur("dve"))])
            veop("vector", "tensor_scalar", glD[:, t8], glC[:, t8],
                 gm1[:, tc:tc + 1], None, ALU.is_equal, deps=[("dve", cur("dve"))])
            veop("vector", "tensor_sub", gg0[:, tc:tc + 1], gm0[:, tc:tc + 1],
                 gm1[:, tc:tc + 1], deps=[("dve", cur("dve"))])
            veop("scalar", "activation", gg1[:, tc:tc + 1], gg0[:, tc:tc + 1],
                 AF.Sigmoid, deps=[("dve", cur("dve"))])
            veop("vector", "tensor_scalar", gg0[:, tc:tc + 1], gg1[:, tc:tc + 1],
                 -1.0, 1.0, ALU.mult, ALU.add, deps=[("act", cur("act"))])
            veop("vector", "tensor_scalar", glC[:, t8], glB[:, t8],
                 gg1[:, tc:tc + 1], None, ALU.mult, deps=[("act", cur("act")), ("dve", cur("dve"))])
            veop("vector", "scalar_tensor_tensor", glA[:, t8], glD[:, t8],
                 gg0[:, tc:tc + 1], glC[:, t8], ALU.mult, ALU.add,
                 deps=[("dve", cur("dve"))])
            veop("scalar", "tensor_copy", glbf[:, t8], glA[:, t8],
                 deps=[("dve", cur("dve"))], key="glb")
        mm_group(5, [(PB[5][0:8, tc * 128:(tc + 1) * 128], glbf[:, tc * 8:(tc + 1) * 8],
                      ident[:, :]) for tc in range(4)], deps=["glb", "init"])
        evac("scalar", 5, "tensor_copy", rows8b[:, :], PB[5][0:8, 0:512], key="cbrow")
        tap("cbrow", st, rows8b[:, :])
        for e in range(EXP):
            ping = e % 2
            mm_group(5, [(PB[5][:, 0:512], rs8(e), rows8b[:, :])], deps=["cbrow"])
            evac("vector", 5, "tensor_copy", cbbc[:, ping * 512:(ping + 1) * 512],
                 PB[5][:, 0:512], deps=[f"cbdead{ping}"], key=f"cb{e}")
            for fs in range(4):
                si = (e * 4 + fs) % 2
                dma("sync", wsl[si][:, 0:8192], sc_w1[e * 4 + fs],
                    key=f"w1_{e}_{fs}", sem="wld",
                    deps=[f"scw1_{e}_{fs}"] + ([slot_last[si]] if slot_last[si] else []))
                for fl in range(8):
                    fc = fs * 8 + fl
                    b = fc % 2
                    mm_group(b, [(PB[b][:, 0:512],
                                  wsl[si][:, k * 1024 + fl * 128: k * 1024 + fl * 128 + 128],
                                  midv(k)) for k in range(NCH)],
                             deps=[f"w1_{e}_{fs}", "mid"])
                    evac("scalar", b, "activation", hview(fc), PB[b][:, 0:512],
                         AF.Relu, deps=["ls_dead", "ll_dead"], key="h")
                slot_last[si] = ("pe", cur("pe"))
            def w2load(do):
                wsi = 10 + (do % 2)
                dma("sync", w2sl[do % 2][:, 0:4096], sc_w2[e * 8 + do],
                    key=f"w2_{e}_{do}", sem="wld",
                    deps=["convB", "convB2", f"scw2_{e}_{do}"] +
                         ([slot_last[wsi]] if slot_last[wsi] else []))
            w2load(0); w2load(1)
            for do in range(NCH):
                b = 2 + (do % 2)
                mm_group(b, [(PB[b][:, 0:512], w2sl[do % 2][:, k * 128:(k + 1) * 128],
                              hview(k)) for k in range(32)],
                         deps=[f"w2_{e}_{do}", "h"])
                slot_last[10 + (do % 2)] = ("pe", cur("pe"))
                if do + 2 < NCH:
                    w2load(do + 2)
                if e == 0:
                    evac("vector", b, "tensor_mul", spool[:, do * 512:(do + 1) * 512],
                         PB[b][:, 0:512], cbbc[:, ping * 512:(ping + 1) * 512],
                         deps=[f"cb{e}"], key="y")
                else:
                    evac("vector", b, "tensor_mul", xf32[:, 512:1024],
                         PB[b][:, 0:512], cbbc[:, ping * 512:(ping + 1) * 512],
                         deps=[f"cb{e}", "xfdead1"], key="yt")
                    veop("vector", "tensor_add", spool[:, do * 512:(do + 1) * 512],
                         spool[:, do * 512:(do + 1) * 512], xf32[:, 512:1024],
                         deps=["yt"], key="y")
                    mark("xfdead1", "dve")
            mark(f"cbdead{ping}", "pe")
        mark("mid_dead", "pe")
        mark("xbf_used", "pe")
        mark("shA_dead", "pe")
        tap("y", st, spool[:, :])
        for c in range(NCH):
            veop("vector", "tensor_add", spool[:, c * 512:(c + 1) * 512],
                 spool[:, c * 512:(c + 1) * 512], midv(c), deps=["y", "mid"], key="s2")
        ln_rows("s2", spool)
        for c in range(NCH):
            ping = c % 2
            veop("vector", "tensor_sub", xf32[:, 0:512],
                 spool[:, c * 512:(c + 1) * 512], PB[3][:, 0:512],
                 deps=["s2", ("pe", PEG[3]), "xfdead0"])
            mark("xfdead0", "dve")
            veop("vector", "tensor_mul", outstg[:, ping * 512:(ping + 1) * 512],
                 xf32[:, 0:512], PB[4][:, 0:512],
                 deps=[("pe", PEG[4]), f"stg{ping}"], key=f"out{c}")
            dma("sync", out_t[:, c * 128:(c + 1) * 128, t0:t0 + TT].rearrange(
                    "s p t -> p s t"),
                outstg[:, ping * 512:(ping + 1) * 512].rearrange(
                    "p (s t) -> p s t", s=2),
                deps=[f"out{c}"])
            mark(f"stg{ping}", "ios")
        BK[3] = ("dve", cur("dve")); BK[4] = ("dve", cur("dve"))
        mark("y_dead", "dve")

    init_phase()
    for st in range(NST):
        for phf in (phase_proj, phase_delta, phase_fir, phase_stats, phase_gates,
                    phase_og, phase_ln1, phase_ffn):
            PH[0] = phf.__name__.replace("phase_", "")
            phf(st)
    op("sync", "wait_ge_final")
    global _LAST
    _LAST = {"OPS": {k: [(m, kw) for (m, a, kw, w, i) in v] for k, v in OPS.items()},
             "LABELS": LABELS}

    import contextlib
    semobjs = {}
    with contextlib.ExitStack() as ctx:
        for n in SEMS:
            semobjs[n] = ctx.enter_context(nc.semaphore(n))
        block = ctx.enter_context(nc.Block())
        final_ios = cur("ios")

        def make_body(lst, isscalar=False):
            def body(e):
                for (method, args, kw, waits, inc) in lst:
                    if isscalar and method == "tensor_copy":
                        method = "copy"
                    if method == "wait_ge_final":
                        e.wait_ge(semobjs["ios"], final_ios)
                        continue
                    for (s, v) in waits:
                        e.wait_ge(semobjs[s], v)
                    instr = getattr(e, method)(*args, **kw)
                    if inc is not None:
                        instr.then_inc(semobjs[inc[0]], inc[1])
            return body

        with nc.allow_low_precision(reason="bf16 rstd broadcast rows; within tolerance"):
            for engname in ["gpsimd", "sync", "tensor", "scalar", "vector"]:
                getattr(block, engname)(make_body(OPS[engname], engname == "scalar"))
    return nc


_LAST = None


_CACHE = {}


def _consts():
    c = {}
    c["ident_i"] = np.eye(128, dtype=np.float32)
    c["masksl_i"] = np.tril(np.ones((128, 128), np.float32), -1)
    c["masksu_i"] = np.triu(np.ones((128, 128), np.float32), 1)
    c["maskui_i"] = np.triu(np.ones((128, 128), np.float32), 0)
    ss = np.zeros((12, 128, 68), np.float32)
    for g in range(3):
        for h in range(4):
            ss[g * 4 + h, :, (0, 32, 64)[g] + h] = 1.0
    c["sumsel_i"] = ss
    r4 = np.zeros((4, 4, 128), np.float32)
    for h in range(4):
        r4[h, h, :] = 1.0
    c["rowsel4_i"] = r4
    r8 = np.zeros((8, 8, 128), np.float32)
    for e in range(8):
        r8[e, e, :] = 1.0
    c["rowsel8_i"] = r8
    c["ones1_i"] = np.ones((1, 128), np.float32)
    c["onescol_i"] = np.ones((128, 1), np.float32)

    def statc(nb, width, dv, Hn):
        a = np.zeros((4, nb * 5 * width), np.float32)
        for b in range(nb):
            def tile(w):
                return a[:, (b * 5 + w) * width:(b * 5 + w + 1) * width]
            tile(0)[:, 4 * b + 0] = 1.0 / (dv * Hn)
            tile(1)[:, 4 * b + 1] = -1.0 / (dv * dv * Hn)
            tile(2)[:, 4 * b + 1] = 1.0 / (dv * Hn)
            tile(3)[:, 4 * b + 3] = 1.0 / Hn
            tile(4)[:, 4 * b + 2] = 1.0 / (dv * Hn)
        return a
    c["statc4_i"] = statc(4, 16, 256, 4)
    c["statc2_i"] = statc(2, 8, 256, 4)
    c["epsc_i"] = np.broadcast_to(np.array([1e-6, 1e-5], np.float32), (4, 2)).copy()
    return c


def _diag(w, ntap):
    z = np.zeros((8, ntap, 128, 128), np.float32)
    r = np.arange(128)
    wr = w.reshape(8, 128, ntap).transpose(0, 2, 1)
    z[:, :, r, r] = wr
    return z


def make_inputs(inputs, L, ncores):
    f = lambda k: np.asarray(inputs[k], np.float32)
    base = {}
    base["wq"] = f("L0_attn_Wq"); base["wk"] = f("L0_attn_Wk"); base["wv"] = f("L0_attn_Wv")
    base["wo"] = f("L0_attn_Wo")
    wg1 = f("L0_attn_Wg1")
    base["wg1"] = np.ascontiguousarray(wg1[:1024])
    base["wg1b"] = np.ascontiguousarray(wg1[1024:])
    wog1 = f("L0_attn_Wog1")
    base["wog1"] = np.ascontiguousarray(wog1[:1024])
    base["wog1b"] = np.ascontiguousarray(wog1[1024:])
    base["wb"] = f("L0_attn_Wb"); base["wres"] = f("L0_attn_Wres")
    base["wg2"] = f("L0_attn_Wg2")
    base["wog2"] = np.ascontiguousarray(f("L0_attn_Wog2").reshape(4, 128).T)
    base["gwT"] = np.ascontiguousarray(f("L0_ffn_gate_w").T)
    base["cdq"] = _diag(f("L0_attn_wq_conv"), 4)
    base["cdk"] = _diag(f("L0_attn_wk_conv"), 4)
    base["cdv"] = _diag(f("L0_attn_wv_conv"), 4)
    base["cds"] = _diag(f("L0_attn_fir_short").reshape(-1, 5), 5)
    base["cdl"] = _diag(f("L0_attn_fir_long").reshape(-1, 64), 64)
    w1 = f("L0_ffn_w1"); w2 = f("L0_ffn_w2")
    base["w1T"] = np.ascontiguousarray(w1.transpose(0, 2, 1))
    base["w2r"] = np.ascontiguousarray(
        w2.transpose(0, 2, 1).reshape(EXP, FF, 8, 128).transpose(0, 2, 1, 3))
    base["bg1"] = np.ascontiguousarray(f("L0_attn_bg1").reshape(8, 128).T)
    base["bog1"] = np.ascontiguousarray(f("L0_attn_bog1").reshape(4, 128).T)
    base["bg2r"] = np.ascontiguousarray(np.broadcast_to(f("L0_attn_bg2"), (128, 4)))
    base["nw"] = np.ascontiguousarray(f("L0_attn_norm_w").reshape(2, 128).T)
    base["bres_i"] = f("L0_attn_bres").reshape(4, 1)
    base["bog2_i"] = f("L0_attn_bog2").reshape(1, 1)
    base["crl_i"] = f("L0_attn_conv_res_logit").reshape(4, 1)
    base["ltemp_i"] = f("L0_attn_logit_temp").reshape(1, 1)
    base.update(_consts())
    x = f("L0_attn_hidden_states")
    in_maps = []
    for core in range(ncores):
        m = dict(base)
        m["x_t"] = np.ascontiguousarray(x[2 * core: 2 * core + 2].transpose(0, 2, 1))
        in_maps.append(m)
    return in_maps


def kernel(**inputs):
    L = 4096
    if "nc" not in _CACHE:
        _CACHE["nc"] = build(L)
    nc = _CACHE["nc"]
    in_maps = make_inputs(inputs, L, 8)
    res = run_bass_kernel_spmd(nc, in_maps, core_ids=list(range(8)))
    outs = []
    for core in range(8):
        o = res.results[core]["out_t"]
        outs.append(np.ascontiguousarray(np.asarray(o).transpose(0, 2, 1)))
    return np.concatenate(outs, axis=0).astype(np.float32)


# version 09
# speedup vs baseline: 1.1890x; 1.0519x over previous
import os
os.environ.setdefault("GAUGE_NTFF_IGNORE_DMA", "1")
import numpy as np
import ml_dtypes
import concourse.bass as bass
import concourse.mybir as mybir
from concourse.bass_utils import run_bass_kernel_spmd

F32 = mybir.dt.float32
BF16 = mybir.dt.bfloat16
AF = mybir.ActivationFunctionType
ALU = mybir.AluOpType
AX = mybir.AxisListType.X

D = 1024; NCH = 8; H = 4; DK = 256; EXP = 8; FF = 4096
TT = 256
STW = 2 * TT
CH = 128
EPS = 1e-5


def build(L, taps=()):
    NST = L // TT
    tapset = set(taps)
    nc = bass.Bass("TRN2", target_bir_lowering=False, debug=False)

    dri = lambda n, s: nc.dram_tensor(n, s, F32, kind="ExternalInput").ap()
    x_t = dri("x_t", [2, D, L])
    wq = dri("wq", [D, D]); wk = dri("wk", [D, D]); wv = dri("wv", [D, D])
    wo = dri("wo", [D, D])
    wg1 = dri("wg1", [D, D]); wg1b = dri("wg1b", [16, D])
    wog1 = dri("wog1", [D, 512]); wog1b = dri("wog1b", [8, 512])
    wb = dri("wb", [D, H]); wres = dri("wres", [D, H]); wg2 = dri("wg2", [D, 4])
    wog2 = dri("wog2", [128, 4]); gwT = dri("gwT", [D, EXP])
    cdq = dri("cdq", [NCH, 4, 128, 128]); cdk = dri("cdk", [NCH, 4, 128, 128])
    cdv = dri("cdv", [NCH, 4, 128, 128])
    cds = dri("cds", [NCH, 5, 128, 128]); cdl = dri("cdl", [NCH, 64, 128, 128])
    w1T = dri("w1T", [EXP, D, FF])
    w2r = dri("w2r", [EXP, NCH, FF, 128])
    bg1 = dri("bg1", [128, NCH]); bog1 = dri("bog1", [128, 4])
    bg2r = dri("bg2r", [128, 4]); nw = dri("nw", [128, 2])
    bres_i = dri("bres_i", [H, 1]); bog2_i = dri("bog2_i", [1, 1])
    crl_i = dri("crl_i", [H, 1]); ltemp_i = dri("ltemp_i", [1, 1])
    ident_i = dri("ident_i", [128, 128])
    masksl_i = dri("masksl_i", [128, 128]); masksu_i = dri("masksu_i", [128, 128])
    maskui_i = dri("maskui_i", [128, 128])
    sumsel_i = dri("sumsel_i", [12, 128, 68])
    rowsel4_i = dri("rowsel4_i", [4, 4, 128]); rowsel8_i = dri("rowsel8_i", [8, 8, 128])
    ones1_i = dri("ones1_i", [1, 128]); onescol_i = dri("onescol_i", [128, 1])
    statc4_i = dri("statc4_i", [4, 4 * 5 * 16])
    statc2_i = dri("statc2_i", [4, 2 * 5 * 8])
    epsc_i = dri("epsc_i", [4, 2])

    out_t = nc.dram_tensor("out_t", [2, D, L], F32, kind="ExternalOutput").ap()
    drs = lambda n, s: nc.dram_tensor(n, s, BF16, kind="Internal").ap()
    sc_attn = drs("sc_attn", [5, 128, 8192])
    sc_wog1 = drs("sc_wog1", [128, 4096])
    sc_cdqkv = drs("sc_cdqkv", [3, 128, 4096])
    sc_cdl = drs("sc_cdl", [8, 128, 8192])
    sc_cds = drs("sc_cds", [8, 128, 640])
    sc_w1 = drs("sc_w1", [EXP * 4, 128, 8192])
    sc_w2 = drs("sc_w2", [EXP * 8, 128, 4096])
    tap_drams = {}

    sb = lambda n, p, w, dt: nc.alloc_sbuf_tensor(n, [p, w], dt)
    ident = sb("ident", 128, 128, BF16)
    masksl = sb("masksl", 128, 128, BF16); masksu = sb("masksu", 128, 128, BF16)
    maskui = sb("maskui", 128, 128, BF16)
    sumsel = sb("sumsel", 128, 12 * 68, BF16)
    rowsel4 = sb("rowsel4", 4, 4 * 128, BF16); rowsel8 = sb("rowsel8", 8, 8 * 128, BF16)
    ones1f = sb("ones1f", 1, 128, F32)
    onescol = sb("onescol", 128, 1, BF16)
    statc4 = sb("statc4", 4, 4 * 5 * 16, BF16); statc2 = sb("statc2", 4, 2 * 5 * 8, BF16)
    epsc = sb("epsc", 4, 2, F32)
    wb_s = sb("wb_s", 128, NCH * 4, BF16); wres_s = sb("wres_s", 128, NCH * 4, BF16)
    wg2_s = sb("wg2_s", 128, NCH * 4, BF16); gwT_s = sb("gwT_s", 128, NCH * 8, BF16)
    wog2_s = sb("wog2_s", 128, 4, BF16)
    wg1b_s = sb("wg1b_s", 16, D, BF16); wog1b_s = sb("wog1b_s", 8, 512, BF16)
    bg1_s = sb("bg1_s", 128, NCH, F32); bog1_s = sb("bog1_s", 128, 4, F32)
    bg2r_s = sb("bg2r_s", 128, 4, F32); nw_s = sb("nw_s", 128, 2, F32)
    bres_s = sb("bres_s", H, 1, F32); bog2_s = sb("bog2_s", 1, 1, F32)
    crl_s = sb("crl_s", H, 1, F32); ltemp_s = sb("ltemp_s", 1, 1, F32)
    static_s = sb("static_s", H, 1, F32)
    tinv_s = sb("tinv_s", 1, 1, F32); tinv_bc = sb("tinv_bc", 128, 1, F32)
    tmp11 = sb("tmp11", 1, 1, F32); tmp11b = sb("tmp11b", 1, 1, F32)
    tmp11c = sb("tmp11c", 1, 1, F32); tmp11d = sb("tmp11d", 1, 1, F32)
    S_f = sb("S_f", 128, 8 * 512, F32)
    S_b = sb("S_b", 128, 8 * 512, BF16)
    rawhalo = sb("rawhalo", 128, 3 * 2 * NCH * 3, BF16)
    WSW = 69 * 128
    wsl = [sb("ws0", 128, WSW, BF16), sb("ws1", 128, WSW, BF16)]
    w2sl = [sb("w2s0", 128, 32 * 128, BF16), sb("w2s1", 128, 32 * 128, BF16)]
    x_bf = sb("x_bf", 128, NCH * STW, BF16)
    v_in = sb("v_in", 128, NCH * 2 * (63 + TT), BF16)
    shareA = sb("shareA", 128, 2 * NCH * STW, BF16)
    o_sl = sb("o_sl", 128, NCH * STW, BF16)
    rawg1 = sb("rawg1", 128, NCH * 2 * 259, BF16)
    mid_bf = sb("mid_bf", 128, NCH * STW, BF16)
    sqs = sb("sqs", 128, 4 * 512, BF16)
    spool = sb("spool", 128, NCH * 512, F32)
    xf32 = sb("xf32", 128, 2 * 512, F32)
    outstg = sb("outstg", 128, 2 * 512, F32)
    cbbc = sb("cbbc", 128, 2 * 512, BF16)
    rowsAf = sb("rowsAf", 4, 2048, F32)
    rowsAb = sb("rowsAb", 4, 4 * 512, BF16)
    rowsFb = sb("rowsFb", 4, 5 * 512, BF16)
    rows16 = sb("rows16", 16, 512, BF16)
    rows8b = sb("rows8b", 8, 512, BF16)
    rows4f = sb("rows4f", 4, 512, F32)
    pmscr = sb("pmscr", 128, 16, F32); pmscr2 = sb("pmscr2", 128, 16, F32)
    pmscr3 = sb("pmscr3", 128, 16, F32)
    pmbf = sb("pmbf", 128, 16, BF16)
    glA = sb("glA", 128, 32, F32); glB = sb("glB", 128, 32, F32)
    glC = sb("glC", 128, 32, F32); glD = sb("glD", 128, 32, F32)
    glbf = sb("glbf", 128, 32, BF16)
    gm0 = sb("gm0", 128, 4, F32); gm1 = sb("gm1", 128, 4, F32)
    gg0 = sb("gg0", 128, 4, F32); gg1 = sb("gg1", 128, 4, F32)
    dX = [[sb(f"dX{g}{p}", 128, 128, BF16) for p in range(2)] for g in range(3)]
    dU = [[sb(f"dU{g}{p}", 128, 128, BF16) for p in range(2)] for g in range(3)]
    dY = [[sb(f"dY{g}{p}", 128, 256, BF16) for p in range(2)] for g in range(3)]
    dAT = [sb(f"dAT{g}", 128, 128, BF16) for g in range(3)]
    dKB = [sb(f"dKB{g}", 128, 256, BF16) for g in range(3)]
    dVH = [sb(f"dVH{g}", 128, 256, BF16) for g in range(3)]
    dKP = [sb(f"dKP{g}", 128, 256, BF16) for g in range(3)]

    PB = [nc.alloc_psum_tensor(f"pb{i}", [128, 512], F32) for i in range(8)]

    OPS = {"tensor": [], "vector": [], "scalar": [], "gpsimd": [], "sync": []}
    SEMS = {n: [0] for n in ["pe", "dve", "act", "ldg", "ldw2", "ldw3", "wld", "ios"]}
    LABELS = {k: [] for k in OPS}
    PH = ["init"]

    def op(eng, method, *args, wait=None, inc=None, **kw):
        OPS[eng].append((method, args, kw, list(wait or []), inc))
        LABELS[eng].append(PH[0])
        if inc is not None:
            SEMS[inc[0]][0] += inc[1]

    def cur(s):
        return SEMS[s][0]

    BK = [None] * 8
    PEG = [0] * 8
    ready = {}

    def mark(key, sem):
        ready[key] = (sem, cur(sem))

    def uses(*keys):
        w = {}
        for k in keys:
            if k is None:
                continue
            s, c = k if isinstance(k, tuple) else ready.get(k, (None, 0))
            if s is None or c <= 0:
                continue
            if c > w.get(s, -1):
                w[s] = c
        return [(s, c) for s, c in w.items()]

    def epoch(bank, items, deps=()):
        w0 = uses(*deps, *(items[0][3]), BK[bank])
        n = len(items)
        for i, (o, l, r, dk) in enumerate(items):
            ww = w0 if i == 0 else (uses(*dk) if dk else None)
            op("tensor", "matmul", o, l, r, start=(i == 0), stop=(i == n - 1),
               wait=ww, inc=(("pe", 1) if i == n - 1 else None))
        PEG[bank] = cur("pe")
        BK[bank] = None

    def mm_group(bank, mms, deps=()):
        epoch(bank, [(o, l, r, []) for (o, l, r) in mms], deps=deps)

    def evac(eng, bank, method, *args, deps=(), key=None, **kw):
        sem = "dve" if eng == "vector" else "act"
        op(eng, method, *args, wait=uses(("pe", PEG[bank]), *deps), inc=(sem, 1), **kw)
        BK[bank] = (sem, cur(sem))
        if key:
            mark(key, sem)

    def veop(eng, method, *args, deps=(), key=None, **kw):
        sem = "dve" if eng == "vector" else "act"
        op(eng, method, *args, wait=uses(*deps), inc=(sem, 1), **kw)
        if key:
            mark(key, sem)

    def dma(queue, dst, src, key=None, deps=(), sem=None, **kw):
        if sem is None:
            sem = "ldg" if queue == "gpsimd" else "ios"
        op(queue, "dma_start", dst, src, wait=uses(*deps), inc=(sem, 16), **kw)
        if key:
            mark(key, sem)

    def xbf(c, lo=0, w=STW):
        return x_bf[:, c * STW + lo: c * STW + lo + w]

    def vin(c, s, lo, w):
        b = c * (2 * (63 + TT)) + s * (63 + TT) + 63 + lo
        return v_in[:, b: b + w]

    def vinr(c, s, lo, w):
        b = c * (2 * (63 + TT)) + s * (63 + TT) + lo
        return v_in[:, b: b + w]

    def vin_st(c):
        b = c * (2 * (63 + TT))
        return v_in[:, b: b + 2 * (63 + TT)].rearrange(
            "p (s t) -> p s t", s=2)[:, :, 63:63 + TT]

    def vin2(c, lo):
        b = c * (2 * (63 + TT))
        return v_in[:, b: b + 2 * (63 + TT)].rearrange(
            "p (s t) -> p s t", s=2)[:, :, lo:lo + TT]

    def shA(idx, c, lo=0, w=STW):
        b = idx * NCH * STW + c * STW + lo
        return shareA[:, b: b + w]

    def osl(c, lo=0, w=STW):
        return o_sl[:, c * STW + lo: c * STW + lo + w]

    def midv(c, lo=0, w=STW):
        return mid_bf[:, c * STW + lo: c * STW + lo + w]

    def hview(fc, lo=0, w=STW):
        if fc < 16:
            return shareA[:, fc * STW + lo: fc * STW + lo + w]
        if fc < 24:
            return o_sl[:, (fc - 16) * STW + lo: (fc - 16) * STW + lo + w]
        return x_bf[:, (fc - 24) * STW + lo: (fc - 24) * STW + lo + w]

    def raww(c, s, lo, w):
        b = c * 518 + s * 259 + lo
        return rawg1[:, b: b + w]

    def raw2(c, lo):
        b = c * 518
        return rawg1[:, b: b + 518].rearrange(
            "p (s t) -> p s t", s=2)[:, :, lo:lo + TT]

    def g1v(m, lo=0, w=512):
        return rawg1[:, m * 512 + lo: m * 512 + lo + w]

    def sel(j):
        return sumsel[:, j * 68:(j + 1) * 68]

    def rs4(h):
        return rowsel4[:, h * 128:(h + 1) * 128]

    def rs8(e):
        return rowsel8[:, e * 128:(e + 1) * 128]

    def rAb(i, lo=0, w=512):
        return rowsAb[:, i * 512 + lo: i * 512 + lo + w]

    def init_phase():
        loads = [
            (ident[:, :], ident_i), (masksl[:, :], masksl_i),
            (masksu[:, :], masksu_i), (maskui[:, :], maskui_i),
            (ones1f[:, :], ones1_i), (onescol[:, :], onescol_i),
            (statc4[:, :], statc4_i), (statc2[:, :], statc2_i), (epsc[:, :], epsc_i),
            (wb_s.rearrange("p (c h) -> p c h", h=4), wb.rearrange("(c p) h -> p c h", p=128)),
            (wres_s.rearrange("p (c h) -> p c h", h=4), wres.rearrange("(c p) h -> p c h", p=128)),
            (wg2_s.rearrange("p (c h) -> p c h", h=4), wg2.rearrange("(c p) h -> p c h", p=128)),
            (gwT_s.rearrange("p (c h) -> p c h", h=8), gwT.rearrange("(c p) h -> p c h", p=128)),
            (wog2_s[:, :], wog2),
            (wg1b_s[:, :], wg1b), (wog1b_s[:, :], wog1b),
            (bg1_s[:, :], bg1), (bog1_s[:, :], bog1), (bg2r_s[:, :], bg2r),
            (nw_s[:, :], nw), (bres_s[:, :], bres_i), (bog2_s[:, :], bog2_i),
            (crl_s[:, :], crl_i), (ltemp_s[:, :], ltemp_i),
        ]
        for j in range(12):
            loads.append((sel(j), sumsel_i[j]))
        for h in range(4):
            loads.append((rs4(h), rowsel4_i[h]))
        for e in range(8):
            loads.append((rs8(e), rowsel8_i[e]))
        for d, s in loads:
            dma("gpsimd", d, s)
        mark("init", "ldg")
        veop("vector", "memset", S_f[:, :], 0.0, key="S_f")
        veop("vector", "memset", S_b[:, :], 0.0, key="S_b")
        veop("vector", "memset", rawhalo[:, :], 0.0, key="halo")
        veop("vector", "memset", v_in[:, :], 0.0, key="v")
        veop("scalar", "activation", static_s[:, :], crl_s[:, :], AF.Sigmoid,
             deps=["init"], key="static")
        veop("scalar", "activation", tmp11[:, :], ltemp_s[:, :], AF.Abs, deps=["init"])
        veop("scalar", "activation", tmp11b[:, :], tmp11[:, :], AF.Exp, scale=-1.0)
        veop("scalar", "activation", tmp11c[:, :], tmp11b[:, :], AF.Ln, bias=1.0)
        veop("scalar", "activation", tmp11d[:, :], ltemp_s[:, :], AF.Relu, deps=["init"])
        veop("vector", "tensor_add", tmp11[:, :], tmp11c[:, :], tmp11d[:, :],
             deps=[("act", cur("act"))])
        veop("vector", "reciprocal", tinv_s[:, :], tmp11[:, :],
             deps=[("dve", cur("dve"))], key="tinv")
        mm_group(7, [(PB[7][0:128, 0:1], ones1f[:, :], tinv_s[:, :])],
                 deps=["tinv", "init"])
        evac("vector", 7, "tensor_copy", tinv_bc[:, :], PB[7][0:128, 0:1], key="tinvbc")
        itemsA = [
            ("scwq", sc_attn[0], wq.rearrange("(kc p) m -> p kc m", p=128)),
            ("sccdq", sc_cdqkv[0], cdq.rearrange("c t p q -> p c t q")),
            ("scwk", sc_attn[1], wk.rearrange("(kc p) m -> p kc m", p=128)),
            ("sccdk", sc_cdqkv[1], cdk.rearrange("c t p q -> p c t q")),
            ("scwv", sc_attn[2], wv.rearrange("(kc p) m -> p kc m", p=128)),
            ("sccdv", sc_cdqkv[2], cdv.rearrange("c t p q -> p c t q")),
        ]
        for c in range(NCH):
            itemsA.append((f"sccdl{c}", sc_cdl[c], cdl[c].rearrange("t p q -> p t q")))
            itemsA.append((f"sccds{c}", sc_cds[c], cds[c].rearrange("t p q -> p t q")))
        itemsA += [
            ("scwg1", sc_attn[3], wg1.rearrange("(kc p) m -> p kc m", p=128)),
            ("scwog1", sc_wog1, wog1.rearrange("(kc p) m -> p kc m", p=128)),
            ("scwo", sc_attn[4], wo.rearrange("(kc p) m -> p kc m", p=128)),
        ]
        itemsB1 = []
        for e in range(EXP):
            for fs in range(4):
                itemsB1.append((f"scw1_{e}_{fs}", e * 4 + fs))
        itemsB2 = []
        for e in range(EXP):
            for do in range(NCH):
                itemsB2.append((f"scw2_{e}_{do}", e * 8 + do))

        def emitA(item):
            nm, dst, srcap = item
            si = emitA.si
            emitA.si = 1 - si
            shp = list(srcap.shape)
            ncols = 1
            for dd in shp[1:]:
                ncols *= dd
            slotv = wsl[si][:, 0:ncols]
            if len(shp) == 3:
                slotv = slotv.rearrange("p (a b) -> p a b", a=shp[1])
            elif len(shp) == 4:
                slotv = slotv.rearrange("p (a b c) -> p a b c", a=shp[1], b=shp[2])
            dma("gpsimd", slotv, srcap, deps=[("ldg", cur("ldg"))])
            dma("gpsimd", dst, wsl[si][:, 0:ncols], deps=[("ldg", cur("ldg"))])
            mark(nm, "ldg")
        emitA.si = 0

        def emitB1(item):
            nm, idx = item
            e, fs = idx // 4, idx % 4
            for hh in range(2):
                srcap = w1T[e][hh * 512:(hh + 1) * 512,
                               fs * 1024:(fs + 1) * 1024].rearrange(
                                   "(kc p) f -> p kc f", p=128)
                dma("gpsimd", w2sl[0][:, 0:4096].rearrange("p (a b) -> p a b", a=4),
                    srcap, deps=[("ldw2", cur("ldw2"))], sem="ldw2")
                dma("gpsimd", sc_w1[idx][:, hh * 4096:(hh + 1) * 4096],
                    w2sl[0][:, 0:4096], deps=[("ldw2", cur("ldw2"))], sem="ldw2")
            mark(nm, "ldw2")

        def emitB2(item):
            nm, idx = item
            e, do = idx // 8, idx % 8
            dma("gpsimd", w2sl[1][:, 0:4096].rearrange("p (a b) -> p a b", a=32),
                w2r[e, do].rearrange("(kc p) m -> p kc m", p=128),
                deps=[("ldw3", cur("ldw3"))], sem="ldw3")
            dma("gpsimd", sc_w2[idx], w2sl[1][:, 0:4096],
                deps=[("ldw3", cur("ldw3"))], sem="ldw3")
            mark(nm, "ldw3")

        qa, qb1, qb2 = list(itemsA), list(itemsB1), list(itemsB2)
        while qa or qb1 or qb2:
            if qa:
                emitA(qa.pop(0))
            if qb1:
                emitB1(qb1.pop(0))
            if qb2:
                emitB2(qb2.pop(0))
        mark("convA", "ldg")
        mark("convB", "ldw2")
        mark("convB2", "ldw3")

    slot_last = {0: None, 1: None, 10: None, 11: None}

    def tap(name, st, ap):
        if name not in tapset or st != 0:
            return
        shp = list(ap.shape)
        d = nc.dram_tensor(f"tap_{name}", shp, F32, kind="ExternalOutput").ap()
        tap_drams[name] = d
        dma("gpsimd", d, ap,
            deps=[("pe", cur("pe")), ("dve", cur("dve")), ("act", cur("act")),
                  ("ios", cur("ios"))])

    def phase_proj(st):
        t0 = st * TT
        for c in range(NCH):
            for s in range(2):
                veop("vector", "tensor_copy", vinr(c, s, 0, 63), vin(c, s, TT - 63, 63),
                     deps=(["v", "v_dead_pe"] if st > 0 else ["v"]))
        mark("vrot", "dve")
        for c in range(NCH):
            dma("gpsimd", xbf(c).rearrange("p (s t) -> p s t", s=2),
                x_t[:, c * 128:(c + 1) * 128, t0:t0 + TT].rearrange("s p t -> p s t"),
                sem="wld", deps=(["xbf_used"] if c == 0 else []))
        mark("x", "wld")

        projs = [("q", 0, 0), ("k", 1, 1), ("v", 2, 2)]
        for (pn, wi, pidx) in projs:
            dma("gpsimd", wsl[0][:, 0:8192], sc_attn[wi], key="wP", sem="wld",
                deps=["convA", "scw" + pn] + ([slot_last[0]] if slot_last[0] else []))
            dma("gpsimd", wsl[1][:, 0:4 * NCH * 128], sc_cdqkv[pidx], key="cdP",
                sem="wld",
                deps=["convA", "sccd" + pn] + ([slot_last[1]] if slot_last[1] else []))
            for m in range(NCH):
                b = m % 2
                mm_group(b, [(PB[b][:, 0:512],
                              wsl[0][:, k * 1024 + m * 128: k * 1024 + m * 128 + 128],
                              xbf(k)) for k in range(NCH)],
                         deps=["wP", "x"])
                for s in range(2):
                    evac("scalar", b, "tensor_copy", raww(m, s, 3, 256),
                         PB[b][:, s * 256: s * 256 + 256], key=f"raw{m}")
            slot_last[0] = ("pe", cur("pe"))
            for c in range(NCH):
                for s in range(2):
                    hb = pidx * (2 * NCH * 3) + s * (NCH * 3) + c * 3
                    veop("vector", "tensor_copy", raww(c, s, 0, 3),
                         rawhalo[:, hb:hb + 3],
                         deps=["halo", f"raw{c}", ("act", cur("act"))])
                    veop("vector", "tensor_copy", rawhalo[:, hb:hb + 3],
                         raww(c, s, 256, 3), deps=[f"raw{c}"])
            mark("halo", "dve")
            mark("rawfill", "dve")
            for c in range(NCH):
                b = 2 + (c % 2)
                mm_group(b, [(PB[b][:, 0:512],
                              wsl[1][:, (c * 4 + t) * 128:(c * 4 + t) * 128 + 128],
                              raw2(c, t)) for t in range(4)],
                         deps=["cdP", f"raw{c}", "rawfill"])
                if pn == "v":
                    for s in range(2):
                        evac("scalar", b, "activation", vin(c, s, 0, 256),
                             PB[b][:, s * 256:s * 256 + 256], AF.Silu,
                             deps=["vrot", "v_dead_dve"], key="v")
                else:
                    idx = 0 if pn == "q" else 1
                    evac("scalar", b, "activation", shA(idx, c),
                         PB[b][:, 0:512], AF.Silu,
                         deps=["ls_dead", "ll_dead"], key=("qs" if pn == "q" else "ks"))
            slot_last[1] = ("pe", cur("pe"))
        for idx, nm, okey in ((0, "qs", "qn"), (1, "ks", "kn")):
            items = []
            pe_ck = {}
            for c in range(NCH):
                ping = c % 2
                sq_t = sqs[:, ping * 512:(ping + 1) * 512]
                veop("scalar", "activation", sq_t, shA(idx, c), AF.Square,
                     deps=[nm, ("pe", pe_ck.get(c - 2, cur("pe") if c < 2 else 0))],
                     key=f"sq{c}")
                items.append((PB[4][0:68, :], sel(c // 2), sq_t, [f"sq{c}"]))
            w0 = uses(*(items[0][3]), BK[4])
            for i, (o, l, r, dk) in enumerate(items):
                op("tensor", "matmul", o, l, r, start=(i == 0),
                   stop=(i == len(items) - 1),
                   wait=(w0 if i == 0 else uses(*dk)),
                   inc=("pe", 1))
                pe_ck[i] = cur("pe")
            PEG[4] = cur("pe"); BK[4] = None
            evac("scalar", 4, "activation", rows4f[:, 0:512], PB[4][0:4, :], AF.Sqrt,
                 bias=epsc[0:4, 0:1], deps=["init"], key="l2a")
            veop("vector", "reciprocal", rAb(3), rows4f[:, 0:512],
                 deps=["l2a"], key="l2r")
            for h in range(4):
                mm_group(5, [(PB[5][:, 0:512], rs4(h), rAb(3))],
                         deps=["l2r"])
                for cc in (2 * h, 2 * h + 1):
                    evac("vector", 5, "tensor_mul", shA(idx, cc), shA(idx, cc),
                         PB[5][:, 0:512], deps=[nm], key=okey)
        mms = [(PB[6][0:4, :], wb_s[:, k * 4:(k + 1) * 4], xbf(k)) for k in range(NCH)]
        mms += [(PB[6][32:36, :], wres_s[:, k * 4:(k + 1) * 4], xbf(k)) for k in range(NCH)]
        mm_group(6, mms, deps=["x", "init"])
        evac("scalar", 6, "activation", rAb(0), PB[6][0:4, :], AF.Sigmoid, key="beta")
        tap("xbf", st, x_bf[:, :])
        tap("qn", st, shareA[:, 0:NCH * STW])
        tap("kn", st, shareA[:, NCH * STW:2 * NCH * STW])
        tap("vin", st, v_in[:, :])
        evac("scalar", 6, "activation", rAb(2), PB[6][32:36, :], AF.Sigmoid,
             bias=bres_s[:, 0:1], key="rg")
        veop("vector", "tensor_scalar", rAb(2), rAb(2), static_s[:, 0:1], None,
             ALU.mult, deps=["rg", "static"], key="rg")
        tap("beta", st, rowsAb[0:4, 0:512])
        tap("rg", st, rowsAb[0:4, 1024:1536])

    def delta_chunk(st, s, h, i):
        g = (s * 4 + h) % 3
        bP, bQ = 2 * g, 2 * g + 1
        oS = (s * 4 + h) * 512
        pc = s * 256 + i * 128

        def knc(k):
            return shA(1, 2 * h + k, pc, 128)

        def qnc(k):
            return shA(0, 2 * h + k, pc, 128)

        def Sb(k, lo=0, w=256):
            return S_b[:, oS + k * 256 + lo: oS + k * 256 + lo + w]

        mm_group(bP, [(PB[bP][:, 0:128], rs4(h), rAb(0, pc, 128))], deps=["beta"])
        for k in range(2):
            evac("vector", bP, "scalar_tensor_tensor", dKB[g][:, k * 128:(k + 1) * 128],
                 knc(k), -1.0, PB[bP][:, 0:128], ALU.mult, ALU.mult,
                 deps=["kn"], key=f"kb{g}")
            evac("vector", bP, "tensor_mul", dVH[g][:, k * 128:(k + 1) * 128],
                 vin(2 * h + k, s, i * 128, 128), PB[bP][:, 0:128],
                 deps=["v"], key=f"vh{g}")
        mm_group(bP, [(PB[bP][:, 0:128], dKB[g][:, k * 128:(k + 1) * 128], knc(k))
                      for k in range(2)] +
                     [(PB[bP][:, 128:256], knc(k), dKB[g][:, k * 128:(k + 1) * 128])
                      for k in range(2)],
                 deps=[f"kb{g}", "kn"])
        evac("vector", bP, "tensor_mul", dX[g][0][:, :], PB[bP][:, 0:128],
             masksl[:, :], key=f"X{g}")
        op("vector", "tensor_mul", dU[g][0][:, :], PB[bP][:, 128:256], masksu[:, :],
           wait=uses(("pe", PEG[bP])), inc=("dve", 1))
        mark(f"U{g}", "dve")
        BK[bP] = ("dve", cur("dve"))
        mm_group(bQ, [(PB[bQ][:, k * 128:(k + 1) * 128],
                       dVH[g][:, k * 128:(k + 1) * 128], ident[:, :])
                      for k in range(2)] +
                     [(PB[bQ][:, 0:256], dKB[g][:, k * 128:(k + 1) * 128], Sb(k))
                      for k in range(2)],
                 deps=[f"kb{g}", f"vh{g}", "S_b"])
        evac("scalar", bQ, "tensor_copy", dY[g][0][:, :], PB[bQ][:, 0:256], key=f"y{g}")
        for j in range(7):
            pj = j % 2
            mm_group(bP, [(PB[bP][:, 0:256], dU[g][pj][:, :], dY[g][pj][:, :]),
                          (PB[bP][:, 0:256], ident[:, :], dY[g][pj][:, :])],
                     deps=[f"U{g}", f"y{g}"])
            evac("scalar", bP, "tensor_copy", dY[g][1 - pj][:, :], PB[bP][:, 0:256],
                 key=f"y{g}")
            if j < 6:
                mm_group(bQ, [(PB[bQ][:, 0:128], dU[g][pj][:, :], dX[g][pj][:, :]),
                              (PB[bQ][:, 128:256], dX[g][pj][:, :], dU[g][pj][:, :])],
                         deps=[f"X{g}", f"U{g}"])
                evac("vector", bQ, "tensor_copy", dX[g][1 - pj][:, :], PB[bQ][:, 0:128],
                     key=f"X{g}")
                op("scalar", "tensor_copy", dU[g][1 - pj][:, :], PB[bQ][:, 128:256],
                   wait=uses(("pe", PEG[bQ])), inc=("act", 1))
                mark(f"U{g}", "act")
                BK[bQ] = ("act", cur("act"))
        ui2 = dY[g][1]
        mm_group(bP, [(PB[bP][:, 0:128], knc(k), qnc(k)) for k in range(2)],
                 deps=["qn", "kn"])
        evac("vector", bP, "tensor_mul", dAT[g][:, :], PB[bP][:, 0:128],
             maskui[:, :], key=f"at{g}")
        mms = []
        for m in range(2):
            mms.append((PB[bQ][:, m * 128:(m + 1) * 128], Sb(0, m * 128, 128), qnc(0)))
            mms.append((PB[bQ][:, m * 128:(m + 1) * 128], Sb(1, m * 128, 128), qnc(1)))
            mms.append((PB[bQ][:, m * 128:(m + 1) * 128],
                        ui2[:, m * 128:(m + 1) * 128], dAT[g][:, :]))
        mm_group(bQ, mms, deps=[f"at{g}", f"y{g}", "S_b", "qn"])
        for m in range(2):
            evac("scalar", bQ, "tensor_copy", osl(2 * h + m, pc, 128),
                 PB[bQ][:, m * 128:(m + 1) * 128], key="delta")
        mm_group(bP, [(PB[bP][:, k * 128:(k + 1) * 128], knc(k), ident[:, :])
                      for k in range(2)], deps=["kn"])
        evac("scalar", bP, "tensor_copy", dKP[g][:, :], PB[bP][:, 0:256], key=f"kp{g}")
        mm_group(bQ, [(PB[bQ][:, k * 256:(k + 1) * 256],
                       dKP[g][:, k * 128:(k + 1) * 128], ui2[:, :]) for k in range(2)],
                 deps=[f"kp{g}", f"y{g}"])
        for k in range(2):
            evac("vector", bQ, "tensor_add", S_f[:, oS + k * 256: oS + k * 256 + 256],
                 S_f[:, oS + k * 256: oS + k * 256 + 256],
                 PB[bQ][:, k * 256:(k + 1) * 256], deps=["S_f", "S_b"], key="S_f")
        for k in range(2):
            veop("scalar", "tensor_copy", Sb(k), S_f[:, oS + k * 256: oS + k * 256 + 256],
                 deps=["S_f"], key="S_b")

    def phase_delta(st):
        for i in range(TT // CH):
            for s in range(2):
                for h in range(4):
                    delta_chunk(st, s, h, i)
        mark("qn_dead", "pe")
        mark("kn_dead", "pe")
        tap("delta", st, o_sl[:, :])
        tap("S", st, S_f[:, :])

    def phase_fir(st):
        for c in range(NCH):
            si = c % 2
            dma("gpsimd", wsl[si][:, 0:64 * 128], sc_cdl[c],
                key=f"fir{c}", sem="wld",
                deps=["convA", f"sccdl{c}"] + ([slot_last[si]] if slot_last[si] else []))
            dma("gpsimd", wsl[si][:, 64 * 128:69 * 128], sc_cds[c], key=f"firs{c}",
                sem="wld",
                deps=["convA", f"sccds{c}"] + ([slot_last[si]] if slot_last[si] else []))
            b = c % 2
            mm_group(b, [(PB[b][:, 0:512], wsl[si][:, t * 128:(t + 1) * 128],
                          vin2(c, t)) for t in range(64)],
                     deps=[f"fir{c}", "v", "vrot"])
            evac("scalar", b, "tensor_copy", shA(1, c), PB[b][:, 0:512],
                 deps=["kn_dead"], key="ll")
            b2 = 2 + (c % 2)
            mm_group(b2, [(PB[b2][:, 0:512],
                           wsl[si][:, (64 + t) * 128:(64 + t + 1) * 128],
                           vin2(c, 59 + t)) for t in range(5)],
                      deps=[f"firs{c}", "v", "vrot"])
            evac("scalar", b2, "tensor_copy", shA(0, c), PB[b2][:, 0:512],
                 deps=["qn_dead"], key="ls")
            slot_last[si] = ("pe", cur("pe"))
            slot_last[10 + (c % 2)] = ("pe", cur("pe"))
        tap("ls", st, shareA[:, 0:NCH * STW])
        tap("ll", st, shareA[:, NCH * STW:2 * NCH * STW])

    def stats_emit(branches, coef, width, outbank, outrows, outkey, outbuf):
        first_epoch = True
        nb = len(branches)
        for b, (bview, bkeys) in enumerate(branches):
            items = []
            pe_ck = {}
            for c in range(NCH):
                ping = c % 2
                sq_t = sqs[:, ping * 1024: ping * 1024 + 512]
                ab_t = sqs[:, ping * 1024 + 512: ping * 1024 + 1024]
                scrdep = ("pe", pe_ck.get(3 * (c - 2) + 2, cur("pe") if c < 2 else 0))
                veop("scalar", "activation", sq_t, bview(c), AF.Square,
                     deps=list(bkeys) + [scrdep], key=f"stsq{c}")
                veop("scalar", "activation", ab_t, bview(c), AF.Abs,
                     deps=list(bkeys), key=f"stab{c}")
                items.append((PB[6][0:68, :], sel(c // 2), bview(c), list(bkeys)))
                items.append((PB[6][0:68, :], sel(4 + c // 2), sq_t, [f"stsq{c}"]))
                items.append((PB[6][0:68, :], sel(8 + c // 2), ab_t, [f"stab{c}"]))
            w0 = uses(*(items[0][3]), BK[6])
            for i, (o, l, r, dk) in enumerate(items):
                op("tensor", "matmul", o, l, r, start=(i == 0),
                   stop=(i == len(items) - 1),
                   wait=(w0 if i == 0 else uses(*dk)), inc=("pe", 1))
                pe_ck[i] = cur("pe")
            PEG[6] = cur("pe"); BK[6] = None
            evac("scalar", 6, "tensor_copy", rowsFb[:, 0:512], PB[6][0:4, :], key="fA")
            op("scalar", "activation", rowsFb[:, 512:1024], PB[6][0:4, :], AF.Square,
               wait=uses(("pe", PEG[6])), inc=("act", 1)); mark("fsA", "act")
            op("scalar", "tensor_copy", rowsFb[:, 1024:1536], PB[6][32:36, :],
               wait=uses(("pe", PEG[6])), inc=("act", 1)); mark("fB", "act")
            op("scalar", "activation", rowsFb[:, 1536:2048], PB[6][32:36, :], AF.Sqrt,
               wait=uses(("pe", PEG[6])), inc=("act", 1)); mark("fsB", "act")
            op("scalar", "tensor_copy", rowsFb[:, 2048:2560], PB[6][64:68, :],
               wait=uses(("pe", PEG[6])), inc=("act", 1)); mark("fC", "act")
            BK[6] = ("act", cur("act"))
            cof = lambda w: coef[:, (b * 5 + w) * width:(b * 5 + w + 1) * width]
            items2 = [(cof(0), rowsFb[:, 0:512], ["fA", "init"]),
                      (cof(1), rowsFb[:, 512:1024], ["fsA"]),
                      (cof(2), rowsFb[:, 1024:1536], ["fB"]),
                      (cof(3), rowsFb[:, 1536:2048], ["fsB"]),
                      (cof(4), rowsFb[:, 2048:2560], ["fC"])]
            for i, (l, r, dk) in enumerate(items2):
                ww = uses(*dk, BK[outbank]) if (first_epoch and i == 0) else uses(*dk)
                st_ = first_epoch and i == 0
                sp_ = (b == nb - 1) and (i == len(items2) - 1)
                op("tensor", "matmul", PB[outbank][0:outrows, :], l, r,
                   start=st_, stop=sp_, wait=ww, inc=("pe", 1))
            first_epoch = False
        PEG[outbank] = cur("pe"); BK[outbank] = None
        evac("scalar", outbank, "tensor_copy", outbuf, PB[outbank][0:outrows, :],
             key=outkey)

    def phase_stats(st):
        branches = [(lambda c: shA(0, c), ["ls"]),
                    (lambda c: shA(1, c), ["ll"]),
                    (lambda c: osl(c), ["delta"]),
                    (lambda c: vin_st(c), ["v"])]
        stats_emit(branches, statc4, 16, 7, 16, "st16", rows16[:, :])
        tap("st16", st, rows16[:, :])

    def phase_gates(st):
        dma("gpsimd", wsl[0][:, 0:8192].rearrange("p (kc m) -> p kc m", kc=8),
            wg1.rearrange("(kc p) m -> p kc m", p=128), key="wg1l", deps=[slot_last[0]] if slot_last[0] else [])
        for m in range(NCH):
            b = m % 2
            mms = [(PB[b][:, 0:512],
                    wsl[0][:, k * 1024 + m * 128: k * 1024 + m * 128 + 128], xbf(k))
                   for k in range(NCH)]
            mms.append((PB[b][:, 0:512], wg1b_s[:, m * 128:(m + 1) * 128], rows16[:, :]))
            mm_group(b, mms, deps=["wg1l", "x", "st16"])
            evac("scalar", b, "activation", g1v(m), PB[b][:, 0:512], AF.Gelu,
                 bias=bg1_s[:, m:m + 1], key="g1")
        slot_last[0] = ("pe", cur("pe"))
        mms = []
        for tc in range(4):
            for k in range(NCH):
                mms.append((PB[2][:, tc * 4:(tc + 1) * 4],
                            g1v(k, tc * 128, 128), wg2_s[:, k * 4:(k + 1) * 4]))
        mm_group(2, mms, deps=["g1", "init"])
        mark("g1_dead", "pe")
        for tc in range(4):
            t4 = slice(tc * 4, (tc + 1) * 4)
            evac("vector", 2, "tensor_add", pmscr[:, t4], PB[2][:, t4], bg2r_s[:, :],
                 deps=["init"])
            veop("vector", "tensor_scalar", pmscr[:, t4], pmscr[:, t4],
                 tinv_bc[:, 0:1], None, ALU.mult, deps=["tinvbc"])
            veop("vector", "tensor_reduce", gm0[:, tc:tc + 1], pmscr[:, t4], AX,
                 ALU.max, deps=[("dve", cur("dve"))])
            veop("vector", "tensor_scalar", pmscr2[:, t4], pmscr[:, t4],
                 gm0[:, tc:tc + 1], None, ALU.subtract, deps=[("dve", cur("dve"))])
            veop("scalar", "activation", pmscr3[:, t4], pmscr2[:, t4], AF.Exp,
                 deps=[("dve", cur("dve"))])
            veop("vector", "tensor_reduce", gm1[:, tc:tc + 1], pmscr3[:, t4], AX,
                 ALU.add, deps=[("act", cur("act"))])
            veop("vector", "reciprocal", gg0[:, tc:tc + 1], gm1[:, tc:tc + 1],
                 deps=[("dve", cur("dve"))])
            veop("vector", "tensor_scalar", pmscr3[:, t4], pmscr3[:, t4],
                 gg0[:, tc:tc + 1], None, ALU.mult, deps=[("dve", cur("dve"))])
            veop("vector", "tensor_scalar_max", pmscr3[:, t4], pmscr3[:, t4], 0.02)
            veop("vector", "tensor_reduce", gg1[:, tc:tc + 1], pmscr3[:, t4], AX,
                 ALU.add, deps=[("dve", cur("dve"))])
            veop("vector", "reciprocal", gm1[:, tc:tc + 1], gg1[:, tc:tc + 1],
                 deps=[("dve", cur("dve"))])
            veop("vector", "tensor_scalar", pmscr3[:, t4], pmscr3[:, t4],
                 gm1[:, tc:tc + 1], None, ALU.mult, deps=[("dve", cur("dve"))])
            veop("scalar", "tensor_copy", pmbf[:, t4], pmscr3[:, t4],
                 deps=[("dve", cur("dve"))], key="pmb")
        tap("pmscr", st, pmscr[:, :])
        tap("pmscr2", st, pmscr3[:, :])
        tap("gm1", st, gm1[:, :])
        tap("tinvbc", st, tinv_bc[:, :])
        mm_group(3, [(PB[3][0:4, tc * 128:(tc + 1) * 128], pmbf[:, tc * 4:(tc + 1) * 4],
                      ident[:, :]) for tc in range(4)], deps=["pmb", "init"])
        evac("scalar", 3, "tensor_copy", rAb(1), PB[3][0:4, 0:512], key="prow")
        for j, bk in ((1, 4), (2, 5), (3, 6)):
            mm_group(bk, [(PB[bk][:, 0:512], rs4(j), rAb(1))], deps=["prow"])
        t1 = sqs[:, 0:512]; t2 = sqs[:, 512:1024]; t3 = sqs[:, 1024:1536]
        for h in range(4):
            mm_group(3, [(PB[3][:, 0:512], rs4(0), rAb(1)),
                         (PB[3][:, 0:512], rs4(h), rAb(2))], deps=["prow", "rg"])
            for cc in (2 * h, 2 * h + 1):
                veop("vector", "tensor_mul", t1, shA(0, cc), PB[3][:, 0:512],
                     deps=["ls", ("pe", PEG[3]), ("pe", cur("pe"))])
                veop("vector", "tensor_mul", t2, shA(1, cc), PB[4][:, 0:512],
                     deps=["ll", ("pe", PEG[4])])
                veop("vector", "tensor_add", t1, t1, t2)
                veop("vector", "tensor_mul", t2, osl(cc), PB[5][:, 0:512],
                     deps=["delta", ("pe", PEG[5])])
                veop("vector", "tensor_mul", t3, vin_st(cc), PB[6][:, 0:512],
                     deps=["v", ("pe", PEG[6])])
                veop("vector", "tensor_add", t2, t2, t3)
                veop("vector", "tensor_add", osl(cc), t1, t2, key="omix")
            BK[3] = ("dve", cur("dve"))
        BK[4] = BK[5] = BK[6] = ("dve", cur("dve"))
        mark("ls_dead", "dve")
        mark("ll_dead", "dve")
        tap("g1", st, rawg1[:, 0:4096])
        tap("prow", st, rowsAb[0:4, 512:1024])
        tap("omix", st, o_sl[:, :])

    def phase_og(st):
        branches = [(lambda c: osl(c), ["omix"]), (lambda c: vin_st(c), ["v"])]
        stats_emit(branches, statc2, 8, 7, 8, "og8", rows8b[:, :])
        mark("v_dead_pe", "pe")
        mark("v_dead_dve", "dve")
        dma("gpsimd", wsl[1][:, 0:4096].rearrange("p (kc m) -> p kc m", kc=8),
            wog1.rearrange("(kc p) m -> p kc m", p=128), key="wog1l", deps=[slot_last[1]] if slot_last[1] else [])
        for m in range(4):
            b = m % 2
            mms = [(PB[b][:, 0:512],
                    wsl[1][:, k * 512 + m * 128: k * 512 + m * 128 + 128], xbf(k))
                   for k in range(NCH)]
            mms.append((PB[b][:, 0:512], wog1b_s[:, m * 128:(m + 1) * 128], rows8b[:, :]))
            mm_group(b, mms, deps=["wog1l", "x", "og8"])
            evac("scalar", b, "activation", g1v(m), PB[b][:, 0:512], AF.Gelu,
                 bias=bog1_s[:, m:m + 1], key="og1")
        slot_last[1] = ("pe", cur("pe"))
        mm_group(2, [(PB[2][0:1, 0:512], wog2_s[:, k:k + 1], g1v(k))
                     for k in range(4)], deps=["og1", "init"])
        mark("xbf_dead", "pe")
        evac("scalar", 2, "activation", rowsAf[0:1, 0:512], PB[2][0:1, 0:512],
             AF.Sigmoid, bias=bog2_s[0:1, 0:1], key="ogr")
        mm_group(3, [(PB[3][:, 0:512], ones1f[:, :], rowsAf[0:1, 0:512])], deps=["ogr"])
        for c in range(NCH):
            evac("vector", 3, "tensor_mul", osl(c), osl(c), PB[3][:, 0:512],
                 deps=["omix"], key="og_o")
        items = []
        pe_ck = {}
        for c in range(NCH):
            ping = c % 2
            sq_t = sqs[:, ping * 512:(ping + 1) * 512]
            veop("scalar", "activation", sq_t, osl(c), AF.Square,
                 deps=["og_o", ("pe", pe_ck.get(c - 2, cur("pe") if c < 2 else 0))],
                 key=f"rsq{c}")
            items.append((PB[4][0:68, :], sel(c // 2), sq_t, [f"rsq{c}"]))
        w0 = uses(*(items[0][3]), BK[4])
        for i, (o, l, r, dk) in enumerate(items):
            op("tensor", "matmul", o, l, r, start=(i == 0), stop=(i == len(items) - 1),
               wait=(w0 if i == 0 else uses(*dk)), inc=("pe", 1))
            pe_ck[i] = cur("pe")
        PEG[4] = cur("pe"); BK[4] = None
        evac("scalar", 4, "activation", rows4f[:, 0:512], PB[4][0:4, :], AF.Sqrt,
             scale=1.0 / 256.0, bias=epsc[0:4, 1:2], deps=["init"], key="rmsa")
        veop("vector", "reciprocal", rAb(3), rows4f[:, 0:512],
             deps=["rmsa"], key="rmsr")
        for h in range(4):
            mm_group(5, [(PB[5][:, 0:512], rs4(h), rAb(3))], deps=["rmsr"])
            for cc in (2 * h, 2 * h + 1):
                evac("vector", 5, "scalar_tensor_tensor", osl(cc), osl(cc),
                     nw_s[:, (cc % 2):(cc % 2) + 1], PB[5][:, 0:512],
                     ALU.mult, ALU.mult, deps=["og_o", "init"], key="on")
        tap("og8", st, rows8b[:, :])
        tap("on", st, o_sl[:, :])

    def ln_rows(srckey, srcbuf):
        items = []
        pe_ck = {}
        for c in range(NCH):
            ping = c % 2
            sb_t = sqs[:, ping * 1024: ping * 1024 + 512]
            sq_t = sqs[:, ping * 1024 + 512: ping * 1024 + 1024]
            scrdep = ("pe", pe_ck.get(2 * (c - 2) + 1, cur("pe") if c < 2 else 0))
            veop("scalar", "tensor_copy", sb_t, srcbuf[:, c * 512:(c + 1) * 512],
                 deps=[srckey, scrdep], key=f"lnc{c}")
            veop("scalar", "activation", sq_t, sb_t, AF.Square, key=f"lns{c}")
            items.append((PB[2][0:1, :], onescol[:, :], sb_t, [f"lnc{c}", "init"]))
            items.append((PB[2][32:33, :], onescol[:, :], sq_t, [f"lns{c}"]))
        w0 = uses(*(items[0][3]), BK[2])
        for i, (o, l, r, dk) in enumerate(items):
            op("tensor", "matmul", o, l, r, start=(i == 0), stop=(i == len(items) - 1),
               wait=(w0 if i == 0 else uses(*dk)), inc=("pe", 1))
            pe_ck[i] = cur("pe")
        PEG[2] = cur("pe"); BK[2] = None
        mean_r = rowsAf[0:1, 0:512]; msq_r = rowsAf[0:1, 512:1024]
        var_r = rowsAf[0:1, 1024:1536]; rstd_r = rowsAf[0:1, 1536:2048]
        m2_r = rows4f[0:1, 0:512]; sd_r = rowsAf[0:1, 512:1024]
        evac("vector", 2, "tensor_scalar", mean_r, PB[2][0:1, :], 1.0 / D, None,
             ALU.mult, key="lnm")
        op("vector", "tensor_scalar", msq_r, PB[2][32:33, :], 1.0 / D, None, ALU.mult,
           wait=uses(("pe", PEG[2])), inc=("dve", 1))
        BK[2] = ("dve", cur("dve"))
        veop("vector", "tensor_mul", m2_r, mean_r, mean_r,
             deps=[("dve", cur("dve"))])
        veop("vector", "tensor_sub", var_r, msq_r, m2_r,
             deps=[("dve", cur("dve"))])
        veop("scalar", "activation", sd_r, var_r, AF.Sqrt, bias=epsc[0:1, 1:2],
             deps=[("dve", cur("dve")), "init"])
        veop("vector", "reciprocal", rstd_r, sd_r,
             deps=[("act", cur("act")), ("dve", cur("dve"))], key="lnr")
        mm_group(3, [(PB[3][:, 0:512], ones1f[:, :], mean_r)], deps=["lnm", "lnr"])
        mm_group(4, [(PB[4][:, 0:512], ones1f[:, :], rstd_r)], deps=["lnr"])

    def phase_ln1(st):
        t0 = st * TT
        dma("gpsimd", wsl[0][:, 0:8192].rearrange("p (kc m) -> p kc m", kc=8),
            wo.rearrange("(kc p) m -> p kc m", p=128), key="wol", deps=[slot_last[0]] if slot_last[0] else [])
        for m in range(NCH):
            b = m % 2
            mm_group(b, [(PB[b][:, 0:512],
                          wsl[0][:, k * 1024 + m * 128: k * 1024 + m * 128 + 128],
                          osl(k)) for k in range(NCH)],
                     deps=["wol", "on"])
            ping = m % 2
            dma("sync", xf32[:, ping * 512:(ping + 1) * 512].rearrange(
                "p (s t) -> p s t", s=2),
                x_t[:, m * 128:(m + 1) * 128, t0:t0 + TT].rearrange("s p t -> p s t"),
                key=f"xf{m}", deps=([f"xfdead{ping}"] if m >= 2 else []))
            evac("vector", b, "tensor_add", spool[:, m * 512:(m + 1) * 512],
                 xf32[:, ping * 512:(ping + 1) * 512], PB[b][:, 0:512],
                 deps=[f"xf{m}"], key="ssum")
            mark(f"xfdead{ping}", "dve")
        slot_last[0] = ("pe", cur("pe"))
        mark("o_dead", "pe")
        tap("ssum", st, spool[:, :])
        ln_rows("ssum", spool)
        for c in range(NCH):
            veop("vector", "tensor_sub", xf32[:, 0:512],
                 spool[:, c * 512:(c + 1) * 512], PB[3][:, 0:512],
                 deps=["ssum", ("pe", PEG[3]), "xfdead0"])
            mark("xfdead0", "dve")
            veop("vector", "tensor_mul", midv(c), xf32[:, 0:512], PB[4][:, 0:512],
                 deps=[("pe", PEG[4]), "mid_dead"], key="mid")
        BK[3] = ("dve", cur("dve")); BK[4] = ("dve", cur("dve"))
        tap("mid", st, mid_bf[:, :])

    def phase_ffn(st):
        t0 = st * TT
        mms = []
        for tc in range(4):
            for k in range(NCH):
                mms.append((PB[5][:, tc * 8:(tc + 1) * 8],
                            midv(k, tc * 128, 128), gwT_s[:, k * 8:(k + 1) * 8]))
        mm_group(5, mms, deps=["mid", "init"])
        for tc in range(4):
            t8 = slice(tc * 8, (tc + 1) * 8)
            evac("vector", 5, "tensor_copy", glA[:, t8], PB[5][:, t8])
            veop("vector", "tensor_reduce", gm0[:, tc:tc + 1], glA[:, t8], AX,
                 ALU.max, deps=[("dve", cur("dve"))])
            veop("vector", "tensor_scalar", glB[:, t8], glA[:, t8],
                 gm0[:, tc:tc + 1], None, ALU.is_equal, deps=[("dve", cur("dve"))])
            veop("vector", "scalar_tensor_tensor", glC[:, t8], glB[:, t8], -1e30,
                 glA[:, t8], ALU.mult, ALU.add, deps=[("dve", cur("dve"))])
            veop("vector", "tensor_reduce", gm1[:, tc:tc + 1], glC[:, t8], AX,
                 ALU.max, deps=[("dve", cur("dve"))])
            veop("vector", "tensor_scalar", glD[:, t8], glC[:, t8],
                 gm1[:, tc:tc + 1], None, ALU.is_equal, deps=[("dve", cur("dve"))])
            veop("vector", "tensor_sub", gg0[:, tc:tc + 1], gm0[:, tc:tc + 1],
                 gm1[:, tc:tc + 1], deps=[("dve", cur("dve"))])
            veop("scalar", "activation", gg1[:, tc:tc + 1], gg0[:, tc:tc + 1],
                 AF.Sigmoid, deps=[("dve", cur("dve"))])
            veop("vector", "tensor_scalar", gg0[:, tc:tc + 1], gg1[:, tc:tc + 1],
                 -1.0, 1.0, ALU.mult, ALU.add, deps=[("act", cur("act"))])
            veop("vector", "tensor_scalar", glC[:, t8], glB[:, t8],
                 gg1[:, tc:tc + 1], None, ALU.mult, deps=[("act", cur("act")), ("dve", cur("dve"))])
            veop("vector", "scalar_tensor_tensor", glA[:, t8], glD[:, t8],
                 gg0[:, tc:tc + 1], glC[:, t8], ALU.mult, ALU.add,
                 deps=[("dve", cur("dve"))])
            veop("scalar", "tensor_copy", glbf[:, t8], glA[:, t8],
                 deps=[("dve", cur("dve"))], key="glb")
        mm_group(5, [(PB[5][0:8, tc * 128:(tc + 1) * 128], glbf[:, tc * 8:(tc + 1) * 8],
                      ident[:, :]) for tc in range(4)], deps=["glb", "init"])
        evac("scalar", 5, "tensor_copy", rows8b[:, :], PB[5][0:8, 0:512], key="cbrow")
        tap("cbrow", st, rows8b[:, :])
        for e in range(EXP):
            ping = e % 2
            mm_group(5, [(PB[5][:, 0:512], rs8(e), rows8b[:, :])], deps=["cbrow"])
            evac("vector", 5, "tensor_copy", cbbc[:, ping * 512:(ping + 1) * 512],
                 PB[5][:, 0:512], deps=[f"cbdead{ping}"], key=f"cb{e}")
            for fs in range(4):
                si = (e * 4 + fs) % 2
                dma("gpsimd", wsl[si][:, 0:8192], sc_w1[e * 4 + fs],
                    key=f"w1_{e}_{fs}", sem="wld",
                    deps=[f"scw1_{e}_{fs}"] + ([slot_last[si]] if slot_last[si] else []))
                for fl in range(8):
                    fc = fs * 8 + fl
                    b = fc % 2
                    mm_group(b, [(PB[b][:, 0:512],
                                  wsl[si][:, k * 1024 + fl * 128: k * 1024 + fl * 128 + 128],
                                  midv(k)) for k in range(NCH)],
                             deps=[f"w1_{e}_{fs}", "mid"])
                    evac("scalar", b, "activation", hview(fc), PB[b][:, 0:512],
                         AF.Relu, deps=["ls_dead", "ll_dead"], key="h")
                slot_last[si] = ("pe", cur("pe"))
            def w2load(do):
                wsi = 10 + (do % 2)
                dma("gpsimd", w2sl[do % 2][:, 0:4096], sc_w2[e * 8 + do],
                    key=f"w2_{e}_{do}", sem="wld",
                    deps=["convB", "convB2", f"scw2_{e}_{do}"] +
                         ([slot_last[wsi]] if slot_last[wsi] else []))
            w2load(0); w2load(1)
            for do in range(NCH):
                b = 2 + (do % 2)
                mm_group(b, [(PB[b][:, 0:512], w2sl[do % 2][:, k * 128:(k + 1) * 128],
                              hview(k)) for k in range(32)],
                         deps=[f"w2_{e}_{do}", "h"])
                slot_last[10 + (do % 2)] = ("pe", cur("pe"))
                if do + 2 < NCH:
                    w2load(do + 2)
                if e == 0:
                    evac("vector", b, "tensor_mul", spool[:, do * 512:(do + 1) * 512],
                         PB[b][:, 0:512], cbbc[:, ping * 512:(ping + 1) * 512],
                         deps=[f"cb{e}"], key="y")
                else:
                    evac("vector", b, "tensor_mul", xf32[:, 512:1024],
                         PB[b][:, 0:512], cbbc[:, ping * 512:(ping + 1) * 512],
                         deps=[f"cb{e}", "xfdead1"], key="yt")
                    veop("vector", "tensor_add", spool[:, do * 512:(do + 1) * 512],
                         spool[:, do * 512:(do + 1) * 512], xf32[:, 512:1024],
                         deps=["yt"], key="y")
                    mark("xfdead1", "dve")
            mark(f"cbdead{ping}", "pe")
        mark("mid_dead", "pe")
        mark("xbf_used", "pe")
        mark("shA_dead", "pe")
        tap("y", st, spool[:, :])
        for c in range(NCH):
            veop("vector", "tensor_add", spool[:, c * 512:(c + 1) * 512],
                 spool[:, c * 512:(c + 1) * 512], midv(c), deps=["y", "mid"], key="s2")
        ln_rows("s2", spool)
        for c in range(NCH):
            ping = c % 2
            veop("vector", "tensor_sub", xf32[:, 0:512],
                 spool[:, c * 512:(c + 1) * 512], PB[3][:, 0:512],
                 deps=["s2", ("pe", PEG[3]), "xfdead0"])
            mark("xfdead0", "dve")
            veop("vector", "tensor_mul", outstg[:, ping * 512:(ping + 1) * 512],
                 xf32[:, 0:512], PB[4][:, 0:512],
                 deps=[("pe", PEG[4]), f"stg{ping}"], key=f"out{c}")
            dma("sync", out_t[:, c * 128:(c + 1) * 128, t0:t0 + TT].rearrange(
                    "s p t -> p s t"),
                outstg[:, ping * 512:(ping + 1) * 512].rearrange(
                    "p (s t) -> p s t", s=2),
                deps=[f"out{c}"])
            mark(f"stg{ping}", "ios")
        BK[3] = ("dve", cur("dve")); BK[4] = ("dve", cur("dve"))
        mark("y_dead", "dve")

    init_phase()
    for st in range(NST):
        for phf in (phase_proj, phase_delta, phase_fir, phase_stats, phase_gates,
                    phase_og, phase_ln1, phase_ffn):
            PH[0] = phf.__name__.replace("phase_", "")
            phf(st)
    op("sync", "wait_ge_final")
    global _LAST
    _LAST = {"OPS": {k: [(m, kw) for (m, a, kw, w, i) in v] for k, v in OPS.items()},
             "LABELS": LABELS}

    import contextlib
    semobjs = {}
    with contextlib.ExitStack() as ctx:
        for n in SEMS:
            semobjs[n] = ctx.enter_context(nc.semaphore(n))
        block = ctx.enter_context(nc.Block())
        final_ios = cur("ios")

        def make_body(lst, isscalar=False):
            def body(e):
                for (method, args, kw, waits, inc) in lst:
                    if isscalar and method == "tensor_copy":
                        method = "copy"
                    if method == "wait_ge_final":
                        e.wait_ge(semobjs["ios"], final_ios)
                        continue
                    for (s, v) in waits:
                        e.wait_ge(semobjs[s], v)
                    instr = getattr(e, method)(*args, **kw)
                    if inc is not None:
                        instr.then_inc(semobjs[inc[0]], inc[1])
            return body

        with nc.allow_low_precision(reason="bf16 rstd broadcast rows; within tolerance"):
            for engname in ["gpsimd", "sync", "tensor", "scalar", "vector"]:
                getattr(block, engname)(make_body(OPS[engname], engname == "scalar"))
    return nc


_LAST = None


_CACHE = {}


def _consts():
    c = {}
    c["ident_i"] = np.eye(128, dtype=np.float32)
    c["masksl_i"] = np.tril(np.ones((128, 128), np.float32), -1)
    c["masksu_i"] = np.triu(np.ones((128, 128), np.float32), 1)
    c["maskui_i"] = np.triu(np.ones((128, 128), np.float32), 0)
    ss = np.zeros((12, 128, 68), np.float32)
    for g in range(3):
        for h in range(4):
            ss[g * 4 + h, :, (0, 32, 64)[g] + h] = 1.0
    c["sumsel_i"] = ss
    r4 = np.zeros((4, 4, 128), np.float32)
    for h in range(4):
        r4[h, h, :] = 1.0
    c["rowsel4_i"] = r4
    r8 = np.zeros((8, 8, 128), np.float32)
    for e in range(8):
        r8[e, e, :] = 1.0
    c["rowsel8_i"] = r8
    c["ones1_i"] = np.ones((1, 128), np.float32)
    c["onescol_i"] = np.ones((128, 1), np.float32)

    def statc(nb, width, dv, Hn):
        a = np.zeros((4, nb * 5 * width), np.float32)
        for b in range(nb):
            def tile(w):
                return a[:, (b * 5 + w) * width:(b * 5 + w + 1) * width]
            tile(0)[:, 4 * b + 0] = 1.0 / (dv * Hn)
            tile(1)[:, 4 * b + 1] = -1.0 / (dv * dv * Hn)
            tile(2)[:, 4 * b + 1] = 1.0 / (dv * Hn)
            tile(3)[:, 4 * b + 3] = 1.0 / Hn
            tile(4)[:, 4 * b + 2] = 1.0 / (dv * Hn)
        return a
    c["statc4_i"] = statc(4, 16, 256, 4)
    c["statc2_i"] = statc(2, 8, 256, 4)
    c["epsc_i"] = np.broadcast_to(np.array([1e-6, 1e-5], np.float32), (4, 2)).copy()
    return c


def _diag(w, ntap):
    z = np.zeros((8, ntap, 128, 128), np.float32)
    r = np.arange(128)
    wr = w.reshape(8, 128, ntap).transpose(0, 2, 1)
    z[:, :, r, r] = wr
    return z


def make_inputs(inputs, L, ncores):
    f = lambda k: np.asarray(inputs[k], np.float32)
    base = {}
    base["wq"] = f("L0_attn_Wq"); base["wk"] = f("L0_attn_Wk"); base["wv"] = f("L0_attn_Wv")
    base["wo"] = f("L0_attn_Wo")
    wg1 = f("L0_attn_Wg1")
    base["wg1"] = np.ascontiguousarray(wg1[:1024])
    base["wg1b"] = np.ascontiguousarray(wg1[1024:])
    wog1 = f("L0_attn_Wog1")
    base["wog1"] = np.ascontiguousarray(wog1[:1024])
    base["wog1b"] = np.ascontiguousarray(wog1[1024:])
    base["wb"] = f("L0_attn_Wb"); base["wres"] = f("L0_attn_Wres")
    base["wg2"] = f("L0_attn_Wg2")
    base["wog2"] = np.ascontiguousarray(f("L0_attn_Wog2").reshape(4, 128).T)
    base["gwT"] = np.ascontiguousarray(f("L0_ffn_gate_w").T)
    base["cdq"] = _diag(f("L0_attn_wq_conv"), 4)
    base["cdk"] = _diag(f("L0_attn_wk_conv"), 4)
    base["cdv"] = _diag(f("L0_attn_wv_conv"), 4)
    base["cds"] = _diag(f("L0_attn_fir_short").reshape(-1, 5), 5)
    base["cdl"] = _diag(f("L0_attn_fir_long").reshape(-1, 64), 64)
    w1 = f("L0_ffn_w1"); w2 = f("L0_ffn_w2")
    base["w1T"] = np.ascontiguousarray(w1.transpose(0, 2, 1))
    base["w2r"] = np.ascontiguousarray(
        w2.transpose(0, 2, 1).reshape(EXP, FF, 8, 128).transpose(0, 2, 1, 3))
    base["bg1"] = np.ascontiguousarray(f("L0_attn_bg1").reshape(8, 128).T)
    base["bog1"] = np.ascontiguousarray(f("L0_attn_bog1").reshape(4, 128).T)
    base["bg2r"] = np.ascontiguousarray(np.broadcast_to(f("L0_attn_bg2"), (128, 4)))
    base["nw"] = np.ascontiguousarray(f("L0_attn_norm_w").reshape(2, 128).T)
    base["bres_i"] = f("L0_attn_bres").reshape(4, 1)
    base["bog2_i"] = f("L0_attn_bog2").reshape(1, 1)
    base["crl_i"] = f("L0_attn_conv_res_logit").reshape(4, 1)
    base["ltemp_i"] = f("L0_attn_logit_temp").reshape(1, 1)
    base.update(_consts())
    x = f("L0_attn_hidden_states")
    in_maps = []
    for core in range(ncores):
        m = dict(base)
        m["x_t"] = np.ascontiguousarray(x[2 * core: 2 * core + 2].transpose(0, 2, 1))
        in_maps.append(m)
    return in_maps


def kernel(**inputs):
    L = 4096
    if "nc" not in _CACHE:
        _CACHE["nc"] = build(L)
    nc = _CACHE["nc"]
    in_maps = make_inputs(inputs, L, 8)
    res = run_bass_kernel_spmd(nc, in_maps, core_ids=list(range(8)))
    outs = []
    for core in range(8):
        o = res.results[core]["out_t"]
        outs.append(np.ascontiguousarray(np.asarray(o).transpose(0, 2, 1)))
    return np.concatenate(outs, axis=0).astype(np.float32)
